# version 05
# baseline (speedup 1.0000x reference)
import numpy as np
import concourse.bass as bass
import concourse.mybir as mybir
from concourse.bass_utils import run_bass_kernel_spmd
from contextlib import ExitStack

F32 = mybir.dt.float32
BF16 = mybir.dt.bfloat16
I32 = mybir.dt.int32
AF = mybir.ActivationFunctionType
ALU = mybir.AluOpType

B, T, D, H, HD = 4, 4096, 1024, 16, 64
TOWN = 2048
NCH = 8
NLCH = 4
DCH = 8
EPS = 1e-5
R = 32
VW = H * (HD + 1)
NG = 8

_STAGE = 99
_TAPS = False
_NOLN4 = False


def _suffix_chunks(g):
    s, out = 256 * g, []
    while s < TOWN:
        w = min(512, TOWN - s)
        out.append((s, w))
        s += w
    return out


def build_nc():
    nc = bass.Bass("TRN2", target_bir_lowering=False, detect_race_conditions=False)
    xt = nc.dram_tensor("xt", [D, T], F32, kind="ExternalInput").ap()
    wq = nc.dram_tensor("wq", [D, D], F32, kind="ExternalInput").ap()
    wk = nc.dram_tensor("wk", [D, D], F32, kind="ExternalInput").ap()
    wv = nc.dram_tensor("wv", [D, D], F32, kind="ExternalInput").ap()
    wo = nc.dram_tensor("wo", [D, D], F32, kind="ExternalInput").ap()
    bqkvo = nc.dram_tensor("bqkvo", [128, 4 * DCH], F32, kind="ExternalInput").ap()
    bvb = nc.dram_tensor("bvb", [128, D], F32, kind="ExternalInput").ap()
    qw = nc.dram_tensor("qw", [6, D, D], I32, kind="ExternalInput").ap()
    sce = nc.dram_tensor("sce", [6, D, D], F32, kind="ExternalInput").ap()
    fb = nc.dram_tensor("fb", [128, 6 * DCH], F32, kind="ExternalInput").ap()
    lat = nc.dram_tensor("lat", [6, D, R], F32, kind="ExternalInput").ap()
    lbt = nc.dram_tensor("lbt", [6, R, D], F32, kind="ExternalInput").ap()
    g3 = nc.dram_tensor("g3", [128, 3 * DCH], F32, kind="ExternalInput").ap()
    be3 = nc.dram_tensor("be3", [128, 3 * DCH], F32, kind="ExternalInput").ap()
    masks = nc.dram_tensor("masks", [NG, 128, 1024], BF16, kind="ExternalInput").ap()
    consts = nc.dram_tensor("consts", [128, 2], F32, kind="ExternalInput").ap()
    out_t = nc.dram_tensor("out_t", [D, TOWN], F32, kind="ExternalOutput").ap()
    kvs = nc.dram_tensor("kvs", [D, T], F32, kind="Internal").ap()
    vvs = nc.dram_tensor("vvs", [T, VW], F32, kind="Internal").ap()
    xr2s = nc.dram_tensor("xr2s", [D, TOWN], F32, kind="Internal").ap()
    dens = nc.dram_tensor("dens", [H, NG, TOWN], F32, kind="Internal").ap()
    taps = {}
    if _TAPS:
        for nm, shp in (("tap_q", [D, TOWN]), ("tap_k", [D, T]), ("tap_v", [T, VW]),
                        ("tap_num", [D, TOWN]), ("tap_den", [16, TOWN]),
                        ("tap_att", [D, TOWN]), ("tap_h0", [D, TOWN]),
                        ("tap_y2", [D, TOWN]), ("tap_xr2", [D, TOWN]),
                        ("tap_wff", [D, D]), ("tap_t1", [R, 512])):
            taps[nm] = nc.dram_tensor(nm, shp, F32, kind="ExternalOutput").ap()

    top = ExitStack()
    with top:
        def mk(stk, n_, s_, d_=F32):
            return stk.enter_context(nc.sbuf_tensor(n_, s_, d_))
        bigB = [mk(top, f"bigB{i}", [128, TOWN]) for i in range(DCH)]
        st = {}
        cb = mk(top, "cb", [128, 4 * DCH])
        cfb = mk(top, "cfb", [128, 6 * DCH])
        cg3 = mk(top, "cg3", [128, 3 * DCH])
        cb3 = mk(top, "cb3", [128, 3 * DCH])
        ccon = mk(top, "ccon", [128, 2])
        ps = [top.enter_context(nc.psum_tensor(f"ps{i}", [128, 512], F32)) for i in range(8)]

        sQ = top.enter_context(nc.semaphore("sQ"))
        sG = top.enter_context(nc.semaphore("sG"))
        sP = top.enter_context(nc.semaphore("sP"))
        sA = top.enter_context(nc.semaphore("sA"))
        sV = top.enter_context(nc.semaphore("sV"))
        sB_ = top.enter_context(nc.semaphore("sB"))
        block = top.enter_context(nc.Block())

        plan = {e: [] for e in ("sync", "gpsimd", "tensor", "vector", "scalar")}
        cnt = {"sQ": 0, "sG": 0, "sP": 0, "sA": 0, "sV": 0, "bar": 0}
        sem = {"sQ": sQ, "sG": sG, "sP": sP, "sA": sA, "sV": sV, "bar": sB_}
        ENGS = ("sync", "gpsimd", "tensor", "vector", "scalar")

        def emit(eng, fn):
            plan[eng].append(fn)

        def dma(eng_name, out, in_):
            sname = "sQ" if eng_name == "sync" else "sG"
            cnt[sname] += 16
            s = sem[sname]
            emit(eng_name, lambda e, o=out, i=in_: e.dma_start(out=o, in_=i).then_inc(s, 16))
            return cnt[sname]

        def wait(eng_name, sem_name, val):
            if val <= 0:
                return
            s = sem[sem_name]
            emit(eng_name, lambda e, v=val: e.wait_ge(s, v))

        def dve(fn, bump=False):
            if bump:
                cnt["sV"] += 1
                emit("vector", lambda v, f=fn: f(v).then_inc(sV, 1))
                return cnt["sV"]
            emit("vector", fn)

        def act(fn, bump=False):
            if bump:
                cnt["sA"] += 1
                emit("scalar", lambda s_, f=fn: f(s_).then_inc(sA, 1))
                return cnt["sA"]
            emit("scalar", fn)

        def pe(fn, bump=False):
            if bump:
                cnt["sP"] += 1
                emit("tensor", lambda t, f=fn: f(t).then_inc(sP, 1))
                return cnt["sP"]
            emit("tensor", fn)

        def pe_drain():
            emit("tensor", lambda t: t.drain())

        def barrier():
            cnt["bar"] += len(ENGS)
            tgt = cnt["bar"]
            for e in ENGS:
                if e == "tensor":
                    emit(e, lambda t: t.drain().then_inc(sB_, 1))
                else:
                    emit(e, lambda t: t.nop().then_inc(sB_, 1))
            for e in ENGS:
                wait(e, "bar", tgt)

        def row(i, w=512):
            return st["rows"][0:1, i * 512:i * 512 + w]

        MU, SS, R1 = range(3)
        R0 = SS
        TMP = R1

        def ln_stats(src_fn, sq_fn, width):
            wait("scalar", "sV", cnt["sV"])
            a_t = 0
            for i in range(DCH):
                a_t = act(lambda s_, i=i: s_.activation(sq_fn(i), src_fn(i), AF.Square),
                          bump=True)
            wait("tensor", "sV", cnt["sV"])
            wait("tensor", "sA", a_t)
            for i in range(DCH):
                pe(lambda t, i=i: t.matmul(ps[6][0:1, 0:width], ccon[:, 1:2], src_fn(i),
                                           start=(i == 0), stop=(i == DCH - 1)))
            p_t = 0
            for i in range(DCH):
                p_t = pe(lambda t, i=i: t.matmul(ps[7][0:1, 0:width], ccon[:, 1:2], sq_fn(i),
                                                 start=(i == 0), stop=(i == DCH - 1)),
                         bump=(i == DCH - 1))
            wait("vector", "sP", p_t)
            dve(lambda v: v.tensor_copy(row(MU, width), ps[6][0:1, 0:width]))
            dve(lambda v: v.tensor_mul(row(TMP, width), row(MU, width), row(MU, width)))
            v_t = dve(lambda v: v.tensor_sub(ps[7][0:1, 0:width], ps[7][0:1, 0:width],
                                             row(TMP, width)), bump=True)
            wait("scalar", "sV", v_t)
            a_t = act(lambda s_: s_.activation(row(SS, width), ps[7][0:1, 0:width], AF.Sqrt,
                                               bias=ccon[0:1, 0:1]), bump=True)
            wait("vector", "sA", a_t)
            dve(lambda v: v.reciprocal(row(R0, width), row(SS, width)))
            dve(lambda v: v.tensor_scalar(ps[7][0:1, 0:width], ps[7][0:1, 0:width],
                                          float(EPS), None, ALU.add))
            dve(lambda v: v.tensor_mul(row(R1, width), row(R0, width), row(R0, width)))
            dve(lambda v: v.tensor_mul(row(R1, width), row(R1, width), ps[7][0:1, 0:width]))
            dve(lambda v: v.tensor_scalar(row(R1, width), row(R1, width), -0.5, 1.5, ALU.mult, ALU.add))
            v_t = dve(lambda v: v.tensor_mul(row(R1, width), row(R1, width), row(R0, width)), bump=True)
            wait("gpsimd", "sV", v_t)
            dma("gpsimd", st["mub"][:, 0:width],
                bass.AP(st["rows"], MU * 512, [[3 * 512, 1], [0, 128], [1, width]]))
            g_t = dma("gpsimd", st["rb"][:, 0:width],
                      bass.AP(st["rows"], R1 * 512, [[3 * 512, 1], [0, 128], [1, width]]))
            return g_t

        dma("sync", cb[:], bqkvo)
        dma("sync", cfb[:], fb)
        dma("sync", cg3[:], g3)
        dma("sync", cb3[:], be3)
        c0 = dma("sync", ccon[:], consts)

        s13 = ExitStack()
        kbl = [mk(s13, f"kbl{i}", [128, 512]) for i in range(DCH)]
        vbl = [mk(s13, f"vbl{i}", [128, VW]) for i in range(4)]
        bigA = [mk(s13, f"bigA{i}", [128, TOWN]) for i in range(DCH)]
        s1 = ExitStack()
        st["rows"] = mk(s1, "rows1", [1, 3 * 512])
        st["mub"] = mk(s1, "mub1", [128, 512])
        st["rb"] = mk(s1, "rb1", [128, 512])
        xch = [mk(s1, f"xch{i}", [128, 512]) for i in range(DCH)]
        wbuf = mk(s1, "wbuf", [128, 2 * 1024])
        wvb = kbl
        cbvb = mk(s1, "cbvb", [128, D])
        cv0 = dma("sync", cbvb[:], bvb)

        if _STAGE >= 1:
            wait("vector", "sQ", c0)
            wait("scalar", "sQ", c0)
            for m in range(NCH):
                wait("sync", "sP", cnt["sP"])
                xw = 0
                for i in range(DCH):
                    xw = dma("sync", xch[i][:], xt[i * 128:(i + 1) * 128, m * 512:(m + 1) * 512])
                wait("scalar", "sQ", xw)
                wait("vector", "sQ", xw)
                wait("scalar", "sG", cnt["sG"])
                g_t = ln_stats(lambda i: xch[i][:], lambda i: kbl[i][:], 512)
                wait("vector", "sG", g_t)
                v_t = 0
                for i in range(DCH):
                    dve(lambda v, i=i: v.tensor_sub(xch[i][:], xch[i][:], st["mub"][:, 0:512]))
                    v_t = dve(lambda v, i=i: v.tensor_mul(xch[i][:], xch[i][:], st["rb"][:, 0:512]),
                              bump=True)
                wait("tensor", "sV", v_t)
                ph = []
                ev = []
                for o in range(DCH):
                    sl = (o % 2) * 1024
                    if len(ph) >= 2:
                        wait("sync", "sP", ph[-2])
                    lw = dma("sync", wbuf[:, sl:sl + 1024],
                             bass.AP(wq.tensor, o * 128, [[D, 128], [128 * D, 8], [1, 128]]))
                    wait("tensor", "sQ", lw)
                    if len(ev) >= 2:
                        wait("tensor", "sV", ev[-2])
                    p_t = 0
                    for i in range(DCH):
                        p_t = pe(lambda t, i=i, o=o, sl=sl: t.matmul(
                            ps[o % 2][:, 0:256], wbuf[:, sl + i * 128:sl + (i + 1) * 128],
                            xch[i][:, 0:256], start=(i == 0), stop=(i == DCH - 1)),
                            bump=(i == DCH - 1))
                    ph.append(p_t)
                    wait("vector", "sP", p_t)
                    ev.append(dve(lambda v, o=o, m=m: v.tensor_scalar(
                        bigA[o][:, m * 256:(m + 1) * 256], ps[o % 2][:, 0:256],
                        cb[:, o:o + 1], None, ALU.add), bump=True))
                ev = []
                for o in range(DCH):
                    sl = (o % 2) * 1024
                    if len(ph) >= 2:
                        wait("sync", "sP", ph[-2])
                    lw = dma("sync", wbuf[:, sl:sl + 1024],
                             bass.AP(wk.tensor, o * 128, [[D, 128], [128 * D, 8], [1, 128]]))
                    wait("tensor", "sQ", lw)
                    if len(ev) >= 2:
                        wait("tensor", "sV", ev[-2])
                    elif o == 0:
                        wait("tensor", "sV", cnt["sV"])
                    p_t = 0
                    for i in range(DCH):
                        p_t = pe(lambda t, i=i, o=o, sl=sl: t.matmul(
                            ps[2 + o % 2][:, 0:512], wbuf[:, sl + i * 128:sl + (i + 1) * 128],
                            xch[i][:, 0:512], start=(i == 0), stop=(i == DCH - 1)),
                            bump=(i == DCH - 1))
                    ph.append(p_t)
                    wait("vector", "sP", p_t)
                    wait("vector", "sG", cnt["sG"] if o == 0 else 0)
                    v_t = dve(lambda v, o=o: v.tensor_scalar(
                        kbl[o][:, 0:512], ps[2 + o % 2][:, 0:512],
                        cb[:, DCH + o:DCH + o + 1], None, ALU.add), bump=True)
                    ev.append(v_t)
                    wait("gpsimd", "sV", v_t)
                    dma("gpsimd", kvs[o * 128:(o + 1) * 128, m * 512:(m + 1) * 512],
                        kbl[o][:, 0:512])
                wait("vector", "sQ", cv0)
                wait("vector", "sG", cnt["sG"])
                z_t = 0
                for tt in range(4):
                    z_t = dve(lambda v, tt=tt: v.memset(
                        bass.AP(vbl[tt], 64, [[VW, 128], [65, 16], [1, 1]]), 1.0), bump=True)
                for hh in range(2):
                    wait("sync", "sP", cnt["sP"])
                    wait("sync", "sG", cnt["sG"])
                    lw = 0
                    for i in range(DCH):
                        lw = dma("sync", wvb[i][:, 0:512],
                                 wv[i * 128:(i + 1) * 128, hh * 512:(hh + 1) * 512])
                    wait("tensor", "sQ", lw)
                    if hh == 0:
                        wait("tensor", "sV", z_t)
                    vev = []
                    for tt in range(4):
                        bank = 4 + (tt % 2)
                        if len(vev) >= 2:
                            wait("tensor", "sV", vev[-2])
                        p_t = 0
                        for i in range(DCH):
                            p_t = pe(lambda t, i=i, tt=tt, bank=bank: t.matmul(
                                ps[bank][:, 0:512], xch[i][:, tt * 128:(tt + 1) * 128],
                                wvb[i][:, 0:512], start=(i == 0), stop=(i == DCH - 1)),
                                bump=(i == DCH - 1))
                        wait("vector", "sP", p_t)
                        vev.append(dve(lambda v, tt=tt, hh=hh, bank=bank: v.tensor_tensor(
                            bass.AP(vbl[tt], hh * 8 * 65, [[VW, 128], [65, 8], [1, 64]]),
                            ps[bank][:, 0:512],
                            cbvb[:, hh * 512:(hh + 1) * 512], ALU.add), bump=True))
                wait("gpsimd", "sV", cnt["sV"])
                for tt in range(4):
                    dma("gpsimd", vvs[m * 512 + tt * 128:m * 512 + (tt + 1) * 128, :], vbl[tt][:])
            barrier()
            if _TAPS:
                g_t = 0
                for i in range(DCH):
                    g_t = dma("gpsimd", taps["tap_q"][i * 128:(i + 1) * 128, :], bigA[i][:])
                g_t = dma("gpsimd", taps["tap_k"], kvs)
                g_t = dma("gpsimd", taps["tap_v"], vvs)
                wait("gpsimd", "sG", g_t)
        s1.close()

        s3 = ExitStack()
        ptl = [mk(s3, f"ptl{i}", [128, 512]) for i in range(16)]
        mbuf = mk(s3, "mbuf", [128, 1024], BF16)
        stage = mk(s3, "stage", [128, 2 * 512])
        if _STAGE >= 3:
            z_t = 0
            for i in range(DCH):
                z_t = dve(lambda v, i=i: v.memset(bigB[i][:], 0.0), bump=True)
            wait("tensor", "sV", z_t)
            stage_hist = []
            for g in range(NG):
                wait("sync", "sP", cnt["sP"])
                kb_t = 0
                for i in range(DCH):
                    kb_t = dma("sync", kbl[i][:],
                               kvs[i * 128:(i + 1) * 128, g * 512:(g + 1) * 512])
                for tt in range(4):
                    kb_t = dma("sync", vbl[tt][:],
                               vvs[g * 512 + tt * 128:g * 512 + (tt + 1) * 128, :])
                mk_t = dma("sync", mbuf[:], masks[g, :, :])
                wait("tensor", "sQ", kb_t)
                wait("vector", "sQ", mk_t)
                for (qs, qw_) in _suffix_chunks(g):
                    first_qc = (qs == 256 * g)
                    for hq in range(4):
                        exps = []
                        for hi in range(4):
                            h = 4 * hq + hi
                            r0 = (h % 2) * 64
                            tile = h // 2
                            for kt in range(4):
                                bank = (4 * hi + kt) % 4
                                if len(exps) >= 4:
                                    wait("tensor", "sA", exps[-4])
                                p_t = pe(lambda t, r0=r0, tile=tile, kt=kt, bank=bank, qs=qs, qw_=qw_:
                                         t.matmul(ps[bank][:, 0:qw_],
                                                  kbl[tile][r0:r0 + 64, kt * 128:(kt + 1) * 128],
                                                  bigA[tile][r0:r0 + 64, qs:qs + qw_],
                                                  start=True, stop=True), bump=True)
                                wait("scalar", "sP", p_t)
                                a_t = act(lambda s_, hi=hi, kt=kt, bank=bank, qw_=qw_:
                                          s_.activation(ptl[4 * hi + kt][:, 0:qw_],
                                                        ps[bank][:, 0:qw_],
                                                        AF.Exp, scale=float(1.0 / 32.0)),
                                          bump=True)
                                exps.append(a_t)
                        gate = ("sA", exps[-1])
                        if first_qc:
                            wait("vector", "sA", exps[-1])
                            m_t = 0
                            for hi in range(4):
                                for kt in range(4):
                                    for qt in range(2):
                                        m_t = dve(lambda v, hi=hi, kt=kt, qt=qt: v.tensor_mul(
                                            ptl[4 * hi + kt][:, qt * 128:(qt + 1) * 128],
                                            ptl[4 * hi + kt][:, qt * 128:(qt + 1) * 128],
                                            mbuf[:, (2 * kt + qt) * 128:(2 * kt + qt + 1) * 128]),
                                            bump=True)
                            gate = ("sV", m_t)
                        pe_drain()
                        for hi in range(4):
                            h = 4 * hq + hi
                            bank = 4 + hi
                            if first_qc:
                                wait("tensor", gate[0], gate[1])
                            else:
                                wait("tensor", "sA", exps[4 * hi + 3])
                            p_t = 0
                            for kt in range(4):
                                p_t = pe(lambda t, h=h, hi=hi, kt=kt, bank=bank, qw_=qw_:
                                         t.matmul(ps[bank][0:65, 0:qw_],
                                                  vbl[kt][:, h * 65:h * 65 + 65],
                                                  ptl[4 * hi + kt][:, 0:qw_],
                                                  start=(kt == 0), stop=(kt == 3)),
                                         bump=(kt == 3))
                            wait("vector", "sP", p_t)
                            dve(lambda v, h=h, bank=bank, qs=qs, qw_=qw_: v.tensor_tensor(
                                bigB[h // 2][(h % 2) * 64:(h % 2) * 64 + 64, qs:qs + qw_],
                                bigB[h // 2][(h % 2) * 64:(h % 2) * 64 + 64, qs:qs + qw_],
                                ps[bank][0:64, 0:qw_], ALU.add), bump=True)
                            slot = len(stage_hist) % 2
                            if len(stage_hist) >= 2:
                                wait("vector", "sG", stage_hist[-2])
                            v_t = dve(lambda v, bank=bank, slot=slot, qw_=qw_: v.tensor_copy(
                                stage[64:65, slot * 512:slot * 512 + qw_],
                                ps[bank][64:65, 0:qw_]), bump=True)
                            wait("gpsimd", "sV", v_t)
                            g_t = dma("gpsimd", dens[h, g, qs:qs + qw_],
                                      bass.AP(stage, 64 * 1024 + slot * 512, [[1024, 1], [1, qw_]]))
                            stage_hist.append(g_t)
                        wait("tensor", "sV", cnt["sV"])
                        pe_drain()
            barrier()
        s3.close()

        s4 = ExitStack()
        st["rows"] = mk(s4, "rows4", [1, 3 * 512])
        st["mub"] = mk(s4, "mub4", [128, 512])
        st["rb"] = mk(s4, "rb4", [128, 512])
        den = mk(s4, "den", [16, TOWN])
        bcst = mk(s4, "bcst", [128, TOWN])
        wob = mk(s4, "wob", [128, 2 * 1024])
        xrt = kbl
        sqt = [vbl[i % 4][:, (i // 4) * 512:(i // 4) * 512 + 512] for i in range(DCH)]
        dtmp = bcst
        if _STAGE >= 4:
            d_t = dma("sync", den[:, :], bass.AP(dens.tensor, 0, [[NG * TOWN, 16], [1, TOWN]]))
            wait("vector", "sQ", d_t)
            for g in range(1, NG):
                w_g = TOWN - 256 * g
                wait("sync", "sV", cnt["sV"])
                d_t = dma("sync", dtmp[0:16, 0:w_g],
                          bass.AP(dens.tensor, g * TOWN + 256 * g, [[NG * TOWN, 16], [1, w_g]]))
                wait("vector", "sQ", d_t)
                dve(lambda v, g=g, w_g=w_g: v.tensor_tensor(
                    den[0:16, 256 * g:TOWN], den[0:16, 256 * g:TOWN],
                    dtmp[0:16, 0:w_g], ALU.add), bump=True)
            v_t = dve(lambda v: v.reciprocal(den[:, :], den[:, :]), bump=True)
            if _TAPS:
                wait("gpsimd", "sV", v_t)
                g_t = 0
                for i in range(DCH):
                    g_t = dma("gpsimd", taps["tap_num"][i * 128:(i + 1) * 128, :], bigB[i][:])
                g_t = dma("gpsimd", taps["tap_den"], den[:, :])
                wait("gpsimd", "sG", g_t)
            wait("gpsimd", "sV", v_t)
            for j in range(DCH):
                wait("gpsimd", "sV", cnt["sV"])
                dma("gpsimd", bcst[0:64, :],
                    bass.AP(den, (2 * j) * TOWN, [[TOWN, 1], [0, 64], [1, TOWN]]))
                b_t = dma("gpsimd", bcst[64:128, :],
                          bass.AP(den, (2 * j + 1) * TOWN, [[TOWN, 1], [0, 64], [1, TOWN]]))
                wait("vector", "sG", b_t)
                dve(lambda v, j=j: v.tensor_mul(bigB[j][:], bigB[j][:], bcst[:]), bump=True)
            att_v = cnt["sV"]
            if _TAPS:
                wait("gpsimd", "sV", att_v)
                g_t = 0
                for i in range(DCH):
                    g_t = dma("gpsimd", taps["tap_att"][i * 128:(i + 1) * 128, :], bigB[i][:])
                wait("gpsimd", "sG", g_t)
            wait("tensor", "sV", att_v)
            for lc in range(NLCH):
                wait("sync", "sG", cnt["sG"])
                wait("sync", "sP", cnt["sP"])
                wait("sync", "sV", cnt["sV"])
                xw = 0
                for i in range(DCH):
                    xw = dma("sync", xrt[i][:],
                             bass.AP(xt.tensor, i * 128 * T + lc * 1024,
                                     [[T, 128], [512, 2], [1, 256]]))
                ph = []
                for o in range(DCH):
                    sl = (o % 2) * 1024
                    if len(ph) >= 2:
                        wait("sync", "sP", ph[-2])
                    lw = dma("sync", wob[:, sl:sl + 1024],
                             bass.AP(wo.tensor, o * 128, [[D, 128], [128 * D, 8], [1, 128]]))
                    wait("tensor", "sQ", lw)
                    if o >= 2:
                        wait("tensor", "sV", cnt["sV"])
                    p_t = 0
                    for i in range(DCH):
                        p_t = pe(lambda t, i=i, o=o, sl=sl, lc=lc: t.matmul(
                            ps[o % 2][:, 0:512], wob[:, sl + i * 128:sl + (i + 1) * 128],
                            bigB[i][:, lc * 512:(lc + 1) * 512],
                            start=(i == 0), stop=(i == DCH - 1)), bump=(i == DCH - 1))
                    ph.append(p_t)
                    if o == 0:
                        wait("vector", "sQ", xw)
                    wait("vector", "sP", p_t)
                    dve(lambda v, o=o: v.tensor_scalar(
                        sqt[o], ps[o % 2][:, 0:512],
                        cb[:, 3 * DCH + o:3 * DCH + o + 1], None, ALU.add))
                    v_t = dve(lambda v, o=o: v.tensor_tensor(
                        xrt[o][:], sqt[o], xrt[o][:], ALU.add), bump=True)
                    wait("gpsimd", "sV", v_t)
                    dma("gpsimd", xr2s[o * 128:(o + 1) * 128, lc * 512:(lc + 1) * 512], xrt[o][:])
                if not _NOLN4:
                    g_t = ln_stats(lambda i: xrt[i][:], lambda i: sqt[i], 512)
                    wait("vector", "sG", g_t)
                    for i in range(DCH):
                        dve(lambda v, i=i: v.tensor_sub(xrt[i][:], xrt[i][:], st["mub"][:, 0:512]))
                        dve(lambda v, i=i, lc=lc: v.tensor_mul(
                            bigA[i][:, lc * 512:(lc + 1) * 512], xrt[i][:], st["rb"][:, 0:512]),
                            bump=True)
            barrier()
            if _TAPS:
                g_t = 0
                for i in range(DCH):
                    g_t = dma("gpsimd", taps["tap_h0"][i * 128:(i + 1) * 128, :], bigA[i][:])
                g_t = dma("gpsimd", taps["tap_xr2"], xr2s)
                wait("gpsimd", "sG", g_t)
        s4.close()

        s5 = ExitStack()
        st["rows"] = mk(s5, "rows5", [1, 3 * 512])
        st["mub"] = mk(s5, "mub5", [128, 512])
        st["rb"] = mk(s5, "rb5", [128, 512])
        wff = [mk(s5, f"wff{i}", [128, 1024]) for i in range(DCH)]
        qwchS = [vbl[0][:, 0:1024].bitcast(I32), vbl[1][:, 0:1024].bitcast(I32)]
        scchS = [vbl[2][:, 0:1024], vbl[3][:, 0:1024]]
        y1 = kbl
        lasbT = mk(s5, "lasbT", [128, DCH * R])
        lbsbT = mk(s5, "lbsbT", [R, DCH * 128])
        lasb = [lasbT[:, i * R:(i + 1) * R] for i in range(DCH)]
        lbsb = [lbsbT[0:R, o * 128:(o + 1) * 128] for o in range(DCH)]
        t1sb = vbl[0][0:R, 0:512]

        def qlora(midx, in_tiles_fn, out_cb):
            wait("sync", "sP", cnt["sP"])
            deq_hist = []
            for o in range(DCH):
                sl = o % 2
                wait("sync", "sV", cnt["sV"])
                lw = dma("sync", qwchS[sl],
                         bass.AP(qw.tensor, midx * D * D + o * 128,
                                 [[D, 128], [128 * D, 8], [1, 128]]))
                lw = dma("sync", scchS[sl],
                         bass.AP(sce.tensor, midx * D * D + o * 128,
                                 [[D, 128], [128 * D, 8], [1, 128]]))
                wait("vector", "sQ", lw)
                if o == 0:
                    wait("vector", "sP", cnt["sP"])
                v_t = 0
                for i in range(DCH):
                    v_t = dve(lambda v, i=i, o=o, sl=sl: v.scalar_tensor_tensor(
                        wff[i][:, o * 128:(o + 1) * 128],
                        qwchS[sl][:, i * 128:(i + 1) * 128], 7.5,
                        scchS[sl][:, i * 128:(i + 1) * 128],
                        ALU.subtract, ALU.mult), bump=True)
                deq_hist.append(v_t)
            wait("sync", "sV", cnt["sV"])
            lw = 0
            for i in range(DCH):
                lw = dma("sync", lasb[i],
                         bass.AP(lat.tensor, midx * D * R + i * 128 * R, [[R, 128], [1, R]]))
            for o in range(DCH):
                lw = dma("sync", lbsb[o],
                         bass.AP(lbt.tensor, midx * R * D + o * 128, [[D, R], [1, 128]]))
            wait("tensor", "sQ", lw)
            wait("tensor", "sV", cnt["sV"])
            if _TAPS and midx == 0:
                wait("gpsimd", "sV", cnt["sV"])
                for i in range(DCH):
                    g_t = dma("gpsimd", taps["tap_wff"][i * 128:(i + 1) * 128, :], wff[i][:])
                wait("gpsimd", "sG", g_t)
            for lc in range(NLCH):
                wait("tensor", "sV", cnt["sV"])
                tiles = in_tiles_fn(lc)
                p_t = 0
                for i in range(DCH):
                    p_t = pe(lambda t, i=i, ap_=tiles[i]: t.matmul(
                        ps[6][0:R, 0:512], lasb[i], ap_,
                        start=(i == 0), stop=(i == DCH - 1)), bump=(i == DCH - 1))
                wait("vector", "sP", p_t)
                v_t = dve(lambda v: v.tensor_scalar(t1sb, ps[6][0:R, 0:512], 7.5, None, ALU.mult),
                          bump=True)
                if _TAPS and midx == 0 and lc == 0:
                    wait("gpsimd", "sV", v_t)
                    g_t2 = dma("gpsimd", taps["tap_t1"], t1sb)
                    wait("gpsimd", "sG", g_t2)
                for o in range(DCH):
                    if o == 6:
                        wait("tensor", "sV", v_t)
                    p_t = 0
                    for i in range(DCH):
                        p_t = pe(lambda t, i=i, o=o, ap_=tiles[i]: t.matmul(
                            ps[o][:, 0:512], wff[i][:, o * 128:(o + 1) * 128], ap_,
                            start=(i == 0), stop=(i == DCH - 1)),
                            bump=(i == DCH - 1 and o == DCH - 1))
                pe_drain()
                wait("tensor", "sV", v_t)
                p_t = 0
                for o in range(DCH):
                    p_t = pe(lambda t, o=o: t.matmul(
                        ps[o][:, 0:512], lbsb[o], t1sb,
                        start=False, stop=True), bump=(o == DCH - 1))
                pe_drain()
                wait("vector", "sP", p_t)
                out_cb(lc, p_t)

        for blk in range(3):
            def in_h(lc):
                return [bigA[i][:, lc * 512:(lc + 1) * 512] for i in range(DCH)]

            def out1(lc, p_t, blk=blk):
                for o in range(DCH):
                    dve(lambda v, o=o, blk=blk: v.tensor_scalar(
                        y1[o][:], ps[o][:, 0:512], float(2.0 / 15.0),
                        cfb[:, (2 * blk) * DCH + o:(2 * blk) * DCH + o + 1],
                        ALU.mult, ALU.add), bump=(o == DCH - 1))
                g_t = ln_stats(lambda i: y1[i][:],
                               lambda i, lc=lc: bigB[i][:, lc * 512:(lc + 1) * 512], 512)
                wait("vector", "sG", g_t)
                v_t = 0
                for i in range(DCH):
                    dve(lambda v, i=i: v.tensor_sub(y1[i][:], y1[i][:], st["mub"][:, 0:512]))
                    v_t = dve(lambda v, i=i: v.tensor_mul(y1[i][:], y1[i][:], st["rb"][:, 0:512]),
                              bump=(i == DCH - 1))
                wait("scalar", "sV", v_t)
                a_t = 0
                for i in range(DCH):
                    a_t = act(lambda s_, i=i, lc=lc, blk=blk: s_.activation(
                        bigB[i][:, lc * 512:(lc + 1) * 512], y1[i][:], AF.Gelu,
                        bias=cb3[:, blk * DCH + i:blk * DCH + i + 1],
                        scale=cg3[:, blk * DCH + i:blk * DCH + i + 1]), bump=(i == DCH - 1))
                wait("tensor", "sA", a_t)

            def in_y2(lc):
                return [bigB[i][:, lc * 512:(lc + 1) * 512] for i in range(DCH)]

            def out2(lc, p_t, blk=blk):
                v_t = 0
                for o in range(DCH):
                    dve(lambda v, o=o, blk=blk: v.tensor_scalar(
                        y1[o][:], ps[o][:, 0:512], float(2.0 / 15.0),
                        cfb[:, (2 * blk + 1) * DCH + o:(2 * blk + 1) * DCH + o + 1],
                        ALU.mult, ALU.add))
                    v_t = dve(lambda v, o=o, lc=lc: v.tensor_tensor(
                        bigA[o][:, lc * 512:(lc + 1) * 512],
                        bigA[o][:, lc * 512:(lc + 1) * 512], y1[o][:], ALU.add), bump=True)
                wait("tensor", "sV", v_t)

            if _STAGE >= 5:
                qlora(2 * blk, in_h, out1)
                qlora(2 * blk + 1, in_y2, out2)
                barrier()
        if _TAPS and _STAGE >= 5:
            g_t = 0
            for i in range(DCH):
                g_t = dma("gpsimd", taps["tap_y2"][i * 128:(i + 1) * 128, :], bigA[i][:])
            wait("gpsimd", "sG", g_t)

        st_t = 0
        if _STAGE >= 5:
            for lc in range(NLCH):
                wait("sync", "sG", cnt["sG"])
                xw = 0
                for i in range(DCH):
                    xw = dma("sync", y1[i][:],
                             xr2s[i * 128:(i + 1) * 128, lc * 512:(lc + 1) * 512])
                wait("vector", "sQ", xw)
                v_t = 0
                for i in range(DCH):
                    v_t = dve(lambda v, i=i, lc=lc: v.tensor_tensor(
                        y1[i][:], y1[i][:], bigA[i][:, lc * 512:(lc + 1) * 512], ALU.add),
                        bump=True)
                wait("gpsimd", "sV", v_t)
                for i in range(DCH):
                    st_t = dma("gpsimd", out_t[i * 128:(i + 1) * 128, lc * 512:(lc + 1) * 512],
                               y1[i][:])
            wait("gpsimd", "sG", st_t)
        else:
            z_t = dve(lambda v: v.memset(bigB[0][:, 0:512], 0.0), bump=True)
            wait("gpsimd", "sV", z_t)
            for i in range(DCH):
                for lc in range(NLCH):
                    st_t = dma("gpsimd", out_t[i * 128:(i + 1) * 128, lc * 512:(lc + 1) * 512],
                               bigB[0][:, 0:512])
            wait("gpsimd", "sG", st_t)
        s5.close()
        s13.close()

        @block.sync
        def _(e):
            for f in plan["sync"]:
                f(e)

        @block.gpsimd
        def _(e):
            for f in plan["gpsimd"]:
                f(e)

        @block.tensor
        def _(e):
            for f in plan["tensor"]:
                f(e)

        @block.vector
        def _(e):
            for f in plan["vector"]:
                f(e)

        @block.scalar
        def _(e):
            for f in plan["scalar"]:
                f(e)

    build_nc._cnt = dict(cnt)
    return nc


def _prep_inputs(inputs):
    x = np.asarray(inputs["L0_attn_x"], dtype=np.float32)
    wqw = np.asarray(inputs["L0_attn_Wq_w"], dtype=np.float32)
    wkw = np.asarray(inputs["L0_attn_Wk_w"], dtype=np.float32)
    wvw = np.asarray(inputs["L0_attn_Wv_w"], dtype=np.float32)
    wow = np.asarray(inputs["L0_attn_Wo_w"], dtype=np.float32)
    bq = np.asarray(inputs["L0_attn_Wq_b"], dtype=np.float32)
    bk = np.asarray(inputs["L0_attn_Wk_b"], dtype=np.float32)
    bv = np.asarray(inputs["L0_attn_Wv_b"], dtype=np.float32)
    bo = np.asarray(inputs["L0_attn_Wo_b"], dtype=np.float32)
    qwq = np.asarray(inputs["L0_ffn_qweights"], dtype=np.int32)
    scl = np.asarray(inputs["L0_ffn_scales"], dtype=np.float32)
    fbi = np.asarray(inputs["L0_ffn_biases"], dtype=np.float32)
    la = np.asarray(inputs["L0_ffn_lora_a"], dtype=np.float32)
    lb = np.asarray(inputs["L0_ffn_lora_b"], dtype=np.float32)
    lg = np.asarray(inputs["L0_ffn_ln_gamma"], dtype=np.float32)
    lbe = np.asarray(inputs["L0_ffn_ln_beta"], dtype=np.float32)

    wq_t = np.ascontiguousarray(wqw.T)
    wk_t = np.ascontiguousarray(wkw.T)
    wv_t = np.ascontiguousarray(wvw.T)
    wo_t = np.ascontiguousarray(wow.T)
    bqkvo = np.ascontiguousarray(np.concatenate([bq, bk, bv, bo]).reshape(4 * DCH, 128).T)
    bvb = np.ascontiguousarray(np.broadcast_to(bv[None, :], (128, D)))
    qw_t = np.ascontiguousarray(qwq.transpose(0, 2, 1))
    sce = np.ascontiguousarray(np.repeat(scl, 16, axis=2).transpose(0, 2, 1)) if scl.shape[2] == 64 else None
    sce = np.ascontiguousarray(np.repeat(scl, 16, axis=2).transpose(0, 2, 1))
    fb_t = np.ascontiguousarray(fbi.reshape(6 * DCH, 128).T)
    lat = np.ascontiguousarray(la.transpose(0, 2, 1))
    lbt = np.ascontiguousarray(lb.transpose(0, 2, 1))
    g3 = np.ascontiguousarray(lg.reshape(3 * DCH, 128).T)
    be3 = np.ascontiguousarray(lbe.reshape(3 * DCH, 128).T)
    consts = np.zeros((128, 2), dtype=np.float32)
    consts[:, 0] = EPS
    consts[:, 1] = 1.0 / D

    import ml_dtypes
    in_maps = []
    for c in range(8):
        b, p = c // 2, c % 2
        perm = np.empty(T, dtype=np.int64)
        for m in range(NCH):
            perm[m * 512:m * 512 + 256] = np.arange((2 * m + p) * 256, (2 * m + p) * 256 + 256)
            perm[m * 512 + 256:(m + 1) * 512] = np.arange((2 * m + 1 - p) * 256,
                                                          (2 * m + 1 - p) * 256 + 256)
        xt_c = np.ascontiguousarray(x[b].T[:, perm])
        mk_ = np.zeros((NG, 128, 1024), dtype=np.float32)
        for g in range(NG):
            kcols = perm[g * 512:(g + 1) * 512]
            qpos = (2 * g + p) * 256 + np.arange(256)
            for kt in range(4):
                for qt in range(2):
                    kk = kcols[kt * 128:(kt + 1) * 128]
                    qq = qpos[qt * 128:(qt + 1) * 128]
                    mk_[g, :, (2 * kt + qt) * 128:(2 * kt + qt + 1) * 128] = (
                        qq[None, :] >= kk[:, None]).astype(np.float32)
        in_maps.append(dict(
            xt=xt_c, wq=wq_t, wk=wk_t, wv=wv_t, wo=wo_t, bqkvo=bqkvo, bvb=bvb,
            qw=qw_t, sce=sce, fb=fb_t, lat=lat, lbt=lbt, g3=g3, be3=be3,
            masks=mk_.astype(ml_dtypes.bfloat16), consts=consts))
    return in_maps


def _gather(results):
    out = np.empty((B, T, D), dtype=np.float32)
    for c in range(8):
        b, p = c // 2, c % 2
        ot = np.asarray(results[c]["out_t"])
        own = np.empty(TOWN, dtype=np.int64)
        for m in range(NCH):
            own[m * 256:(m + 1) * 256] = np.arange((2 * m + p) * 256, (2 * m + p) * 256 + 256)
        out[b, own, :] = ot.T
    return out


def kernel(**inputs):
    in_maps = _prep_inputs(inputs)
    nc = build_nc()
    res = run_bass_kernel_spmd(nc, in_maps, core_ids=list(range(8)))
    return _gather(res.results)


# version 07
# speedup vs baseline: 1.0356x; 1.0356x over previous
import os as _os
import numpy as np
import concourse.bass as bass
import concourse.mybir as mybir
from concourse.bass_utils import run_bass_kernel_spmd
from contextlib import ExitStack

F32 = mybir.dt.float32
BF16 = mybir.dt.bfloat16
I32 = mybir.dt.int32
AF = mybir.ActivationFunctionType
ALU = mybir.AluOpType

B, T, D, H, HD = 4, 4096, 1024, 16, 64
TOWN = 2048
NCH = 8
NLCH = 4
DCH = 8
EPS = 1e-5
R = 32
VW = H * (HD + 1)
NG = 8

_STAGE = 99
_TAPS = False
_NOLN4 = False


def _suffix_chunks(g):
    s, out = 256 * g, []
    while s < TOWN:
        w = min(512, TOWN - s)
        out.append((s, w))
        s += w
    return out


def build_nc():
    nc = bass.Bass("TRN2", target_bir_lowering=False, detect_race_conditions=False)
    xt = nc.dram_tensor("xt", [D, T], F32, kind="ExternalInput").ap()
    wq = nc.dram_tensor("wq", [D, D], F32, kind="ExternalInput").ap()
    wk = nc.dram_tensor("wk", [D, D], F32, kind="ExternalInput").ap()
    wv = nc.dram_tensor("wv", [D, D], F32, kind="ExternalInput").ap()
    wo = nc.dram_tensor("wo", [D, D], F32, kind="ExternalInput").ap()
    bqkvo = nc.dram_tensor("bqkvo", [128, 4 * DCH], F32, kind="ExternalInput").ap()
    bvb = nc.dram_tensor("bvb", [128, D], F32, kind="ExternalInput").ap()
    qw = nc.dram_tensor("qw", [6, D, D], I32, kind="ExternalInput").ap()
    sce = nc.dram_tensor("sce", [6, D, D], F32, kind="ExternalInput").ap()
    fb = nc.dram_tensor("fb", [128, 6 * DCH], F32, kind="ExternalInput").ap()
    lat = nc.dram_tensor("lat", [6, D, R], F32, kind="ExternalInput").ap()
    lbt = nc.dram_tensor("lbt", [6, R, D], F32, kind="ExternalInput").ap()
    g3 = nc.dram_tensor("g3", [128, 3 * DCH], F32, kind="ExternalInput").ap()
    be3 = nc.dram_tensor("be3", [128, 3 * DCH], F32, kind="ExternalInput").ap()
    masks = nc.dram_tensor("masks", [NG, 128, 1024], BF16, kind="ExternalInput").ap()
    consts = nc.dram_tensor("consts", [128, 2], F32, kind="ExternalInput").ap()
    out_t = nc.dram_tensor("out_t", [D, TOWN], F32, kind="ExternalOutput").ap()
    kvs = nc.dram_tensor("kvs", [D, T], F32, kind="Internal").ap()
    vvs = nc.dram_tensor("vvs", [T, VW], F32, kind="Internal").ap()
    xr2s = nc.dram_tensor("xr2s", [D, TOWN], F32, kind="Internal").ap()
    dens = nc.dram_tensor("dens", [H, NG, TOWN], F32, kind="Internal").ap()
    taps = {}
    if _TAPS:
        for nm, shp in (("tap_q", [D, TOWN]), ("tap_k", [D, T]), ("tap_v", [T, VW]),
                        ("tap_num", [D, TOWN]), ("tap_den", [16, TOWN]),
                        ("tap_att", [D, TOWN]), ("tap_h0", [D, TOWN]),
                        ("tap_y2", [D, TOWN]), ("tap_xr2", [D, TOWN]),
                        ("tap_wff", [D, D]), ("tap_t1", [R, 512])):
            taps[nm] = nc.dram_tensor(nm, shp, F32, kind="ExternalOutput").ap()

    top = ExitStack()
    with top:
        def mk(stk, n_, s_, d_=F32):
            return stk.enter_context(nc.sbuf_tensor(n_, s_, d_))
        bigB = [mk(top, f"bigB{i}", [128, TOWN]) for i in range(DCH)]
        st = {}
        cb = mk(top, "cb", [128, 4 * DCH])
        cfb = mk(top, "cfb", [128, 6 * DCH])
        cg3 = mk(top, "cg3", [128, 3 * DCH])
        cb3 = mk(top, "cb3", [128, 3 * DCH])
        ccon = mk(top, "ccon", [128, 2])
        ps = [top.enter_context(nc.psum_tensor(f"ps{i}", [128, 512], F32)) for i in range(8)]

        sQ = top.enter_context(nc.semaphore("sQ"))
        sG = top.enter_context(nc.semaphore("sG"))
        sP = top.enter_context(nc.semaphore("sP"))
        sA = top.enter_context(nc.semaphore("sA"))
        sV = top.enter_context(nc.semaphore("sV"))
        sB_ = top.enter_context(nc.semaphore("sB"))
        block = top.enter_context(nc.Block())

        plan = {e: [] for e in ("sync", "gpsimd", "tensor", "vector", "scalar")}
        cnt = {"sQ": 0, "sG": 0, "sP": 0, "sA": 0, "sV": 0, "bar": 0}
        sem = {"sQ": sQ, "sG": sG, "sP": sP, "sA": sA, "sV": sV, "bar": sB_}
        ENGS = ("sync", "gpsimd", "tensor", "vector", "scalar")

        def emit(eng, fn):
            plan[eng].append(fn)

        def dma(eng_name, out, in_):
            sname = "sQ" if eng_name == "sync" else "sG"
            cnt[sname] += 16
            s = sem[sname]
            emit(eng_name, lambda e, o=out, i=in_: e.dma_start(out=o, in_=i).then_inc(s, 16))
            return cnt[sname]

        def wait(eng_name, sem_name, val):
            if val <= 0:
                return
            s = sem[sem_name]
            emit(eng_name, lambda e, v=val: e.wait_ge(s, v))

        def dve(fn, bump=False):
            if bump:
                cnt["sV"] += 1
                emit("vector", lambda v, f=fn: f(v).then_inc(sV, 1))
                return cnt["sV"]
            emit("vector", fn)

        def act(fn, bump=False):
            if bump:
                cnt["sA"] += 1
                emit("scalar", lambda s_, f=fn: f(s_).then_inc(sA, 1))
                return cnt["sA"]
            emit("scalar", fn)

        def pe(fn, bump=False):
            if bump:
                cnt["sP"] += 1
                emit("tensor", lambda t, f=fn: f(t).then_inc(sP, 1))
                return cnt["sP"]
            emit("tensor", fn)

        def pe_drain():
            emit("tensor", lambda t: t.drain())

        def barrier():
            cnt["bar"] += len(ENGS)
            tgt = cnt["bar"]
            for e in ENGS:
                if e == "tensor":
                    emit(e, lambda t: t.drain().then_inc(sB_, 1))
                else:
                    emit(e, lambda t: t.nop().then_inc(sB_, 1))
            for e in ENGS:
                wait(e, "bar", tgt)

        def row(i, w=512):
            return st["rows"][0:1, i * 512:i * 512 + w]

        MU, SS, R1 = range(3)
        R0 = SS
        TMP = R1

        def ln_stats(src_fn, sq_fn, width):
            wait("scalar", "sV", cnt["sV"])
            a_t = 0
            for i in range(DCH):
                a_t = act(lambda s_, i=i: s_.activation(sq_fn(i), src_fn(i), AF.Square),
                          bump=True)
            wait("tensor", "sV", cnt["sV"])
            wait("tensor", "sA", a_t)
            for i in range(DCH):
                pe(lambda t, i=i: t.matmul(ps[6][0:1, 0:width], ccon[:, 1:2], src_fn(i),
                                           start=(i == 0), stop=(i == DCH - 1)))
            p_t = 0
            for i in range(DCH):
                p_t = pe(lambda t, i=i: t.matmul(ps[7][0:1, 0:width], ccon[:, 1:2], sq_fn(i),
                                                 start=(i == 0), stop=(i == DCH - 1)),
                         bump=(i == DCH - 1))
            wait("vector", "sP", p_t)
            dve(lambda v: v.tensor_copy(row(MU, width), ps[6][0:1, 0:width]))
            dve(lambda v: v.tensor_mul(row(TMP, width), row(MU, width), row(MU, width)))
            v_t = dve(lambda v: v.tensor_sub(ps[7][0:1, 0:width], ps[7][0:1, 0:width],
                                             row(TMP, width)), bump=True)
            wait("scalar", "sV", v_t)
            a_t = act(lambda s_: s_.activation(row(SS, width), ps[7][0:1, 0:width], AF.Sqrt,
                                               bias=ccon[0:1, 0:1]), bump=True)
            wait("vector", "sA", a_t)
            dve(lambda v: v.reciprocal(row(R0, width), row(SS, width)))
            dve(lambda v: v.tensor_scalar(ps[7][0:1, 0:width], ps[7][0:1, 0:width],
                                          float(EPS), None, ALU.add))
            dve(lambda v: v.tensor_mul(row(R1, width), row(R0, width), row(R0, width)))
            dve(lambda v: v.tensor_mul(row(R1, width), row(R1, width), ps[7][0:1, 0:width]))
            dve(lambda v: v.tensor_scalar(row(R1, width), row(R1, width), -0.5, 1.5, ALU.mult, ALU.add))
            v_t = dve(lambda v: v.tensor_mul(row(R1, width), row(R1, width), row(R0, width)), bump=True)
            wait("gpsimd", "sV", v_t)
            dma("gpsimd", st["mub"][:, 0:width],
                bass.AP(st["rows"], MU * 512, [[3 * 512, 1], [0, 128], [1, width]]))
            g_t = dma("gpsimd", st["rb"][:, 0:width],
                      bass.AP(st["rows"], R1 * 512, [[3 * 512, 1], [0, 128], [1, width]]))
            return g_t

        dma("sync", cb[:], bqkvo)
        dma("sync", cfb[:], fb)
        dma("sync", cg3[:], g3)
        dma("sync", cb3[:], be3)
        c0 = dma("sync", ccon[:], consts)

        s13 = ExitStack()
        kbl = [mk(s13, f"kbl{i}", [128, 512]) for i in range(DCH)]
        vbl = [mk(s13, f"vbl{i}", [128, VW]) for i in range(4)]
        bigA = [mk(s13, f"bigA{i}", [128, TOWN]) for i in range(DCH)]
        s1 = ExitStack()
        st["rows"] = mk(s1, "rows1", [1, 3 * 512])
        st["mub"] = mk(s1, "mub1", [128, 512])
        st["rb"] = mk(s1, "rb1", [128, 512])
        xch = [mk(s1, f"xch{i}", [128, 512]) for i in range(DCH)]
        wbuf = mk(s1, "wbuf", [128, 2 * 1024])
        wvb = kbl
        cbvb = mk(s1, "cbvb", [128, D])
        cv0 = dma("sync", cbvb[:], bvb)

        if _STAGE >= 1:
            wait("vector", "sQ", c0)
            wait("scalar", "sQ", c0)
            for m in range(NCH):
                wait("sync", "sP", cnt["sP"])
                xw = 0
                for i in range(DCH):
                    xw = dma("sync", xch[i][:], xt[i * 128:(i + 1) * 128, m * 512:(m + 1) * 512])
                wait("scalar", "sQ", xw)
                wait("vector", "sQ", xw)
                wait("scalar", "sG", cnt["sG"])
                g_t = ln_stats(lambda i: xch[i][:], lambda i: kbl[i][:], 512)
                wait("vector", "sG", g_t)
                v_t = 0
                for i in range(DCH):
                    dve(lambda v, i=i: v.tensor_sub(xch[i][:], xch[i][:], st["mub"][:, 0:512]))
                    v_t = dve(lambda v, i=i: v.tensor_mul(xch[i][:], xch[i][:], st["rb"][:, 0:512]),
                              bump=True)
                wait("tensor", "sV", v_t)
                ph = []
                ev = []
                for o in range(DCH):
                    sl = (o % 2) * 1024
                    if len(ph) >= 2:
                        wait("sync", "sP", ph[-2])
                    lw = dma("sync", wbuf[:, sl:sl + 1024],
                             bass.AP(wq.tensor, o * 128, [[D, 128], [128 * D, 8], [1, 128]]))
                    wait("tensor", "sQ", lw)
                    if len(ev) >= 2:
                        wait("tensor", "sV", ev[-2])
                    p_t = 0
                    for i in range(DCH):
                        p_t = pe(lambda t, i=i, o=o, sl=sl: t.matmul(
                            ps[o % 2][:, 0:256], wbuf[:, sl + i * 128:sl + (i + 1) * 128],
                            xch[i][:, 0:256], start=(i == 0), stop=(i == DCH - 1)),
                            bump=(i == DCH - 1))
                    ph.append(p_t)
                    wait("vector", "sP", p_t)
                    ev.append(dve(lambda v, o=o, m=m: v.tensor_scalar(
                        bigA[o][:, m * 256:(m + 1) * 256], ps[o % 2][:, 0:256],
                        cb[:, o:o + 1], None, ALU.add), bump=True))
                ev = []
                for o in range(DCH):
                    sl = (o % 2) * 1024
                    if len(ph) >= 2:
                        wait("sync", "sP", ph[-2])
                    lw = dma("sync", wbuf[:, sl:sl + 1024],
                             bass.AP(wk.tensor, o * 128, [[D, 128], [128 * D, 8], [1, 128]]))
                    wait("tensor", "sQ", lw)
                    if len(ev) >= 2:
                        wait("tensor", "sV", ev[-2])
                    elif o == 0:
                        wait("tensor", "sV", cnt["sV"])
                    p_t = 0
                    for i in range(DCH):
                        p_t = pe(lambda t, i=i, o=o, sl=sl: t.matmul(
                            ps[2 + o % 2][:, 0:512], wbuf[:, sl + i * 128:sl + (i + 1) * 128],
                            xch[i][:, 0:512], start=(i == 0), stop=(i == DCH - 1)),
                            bump=(i == DCH - 1))
                    ph.append(p_t)
                    wait("vector", "sP", p_t)
                    wait("vector", "sG", cnt["sG"] if o == 0 else 0)
                    v_t = dve(lambda v, o=o: v.tensor_scalar(
                        kbl[o][:, 0:512], ps[2 + o % 2][:, 0:512],
                        cb[:, DCH + o:DCH + o + 1], None, ALU.add), bump=True)
                    ev.append(v_t)
                    wait("gpsimd", "sV", v_t)
                    dma("gpsimd", kvs[o * 128:(o + 1) * 128, m * 512:(m + 1) * 512],
                        kbl[o][:, 0:512])
                wait("vector", "sQ", cv0)
                wait("vector", "sG", cnt["sG"])
                z_t = 0
                for tt in range(4):
                    z_t = dve(lambda v, tt=tt: v.memset(
                        bass.AP(vbl[tt], 64, [[VW, 128], [65, 16], [1, 1]]), 1.0), bump=True)
                for hh in range(2):
                    wait("sync", "sP", cnt["sP"])
                    wait("sync", "sG", cnt["sG"])
                    lw = 0
                    for i in range(DCH):
                        lw = dma("sync", wvb[i][:, 0:512],
                                 wv[i * 128:(i + 1) * 128, hh * 512:(hh + 1) * 512])
                    wait("tensor", "sQ", lw)
                    if hh == 0:
                        wait("tensor", "sV", z_t)
                    vev = []
                    for tt in range(4):
                        bank = 4 + (tt % 2)
                        if len(vev) >= 2:
                            wait("tensor", "sV", vev[-2])
                        p_t = 0
                        for i in range(DCH):
                            p_t = pe(lambda t, i=i, tt=tt, bank=bank: t.matmul(
                                ps[bank][:, 0:512], xch[i][:, tt * 128:(tt + 1) * 128],
                                wvb[i][:, 0:512], start=(i == 0), stop=(i == DCH - 1)),
                                bump=(i == DCH - 1))
                        wait("vector", "sP", p_t)
                        vev.append(dve(lambda v, tt=tt, hh=hh, bank=bank: v.tensor_tensor(
                            bass.AP(vbl[tt], hh * 8 * 65, [[VW, 128], [65, 8], [1, 64]]),
                            ps[bank][:, 0:512],
                            cbvb[:, hh * 512:(hh + 1) * 512], ALU.add), bump=True))
                wait("gpsimd", "sV", cnt["sV"])
                for tt in range(4):
                    dma("gpsimd", vvs[m * 512 + tt * 128:m * 512 + (tt + 1) * 128, :], vbl[tt][:])
            barrier()
            if _TAPS:
                g_t = 0
                for i in range(DCH):
                    g_t = dma("gpsimd", taps["tap_q"][i * 128:(i + 1) * 128, :], bigA[i][:])
                g_t = dma("gpsimd", taps["tap_k"], kvs)
                g_t = dma("gpsimd", taps["tap_v"], vvs)
                wait("gpsimd", "sG", g_t)
        s1.close()

        s3 = ExitStack()
        ptl = [mk(s3, f"ptl{i}", [128, 512]) for i in range(16)]
        mbuf = mk(s3, "mbuf", [128, 1024], BF16)
        stage = mk(s3, "stage", [128, 2 * 512])
        if _STAGE >= 3:
            z_t = 0
            for i in range(DCH):
                z_t = dve(lambda v, i=i: v.memset(bigB[i][:], 0.0), bump=True)
            wait("tensor", "sV", z_t)
            stage_hist = []
            for g in range(NG):
                wait("sync", "sP", cnt["sP"])
                kb_t = 0
                for i in range(DCH):
                    kb_t = dma("sync", kbl[i][:],
                               kvs[i * 128:(i + 1) * 128, g * 512:(g + 1) * 512])
                for tt in range(4):
                    kb_t = dma("sync", vbl[tt][:],
                               vvs[g * 512 + tt * 128:g * 512 + (tt + 1) * 128, :])
                mk_t = dma("sync", mbuf[:], masks[g, :, :])
                wait("tensor", "sQ", kb_t)
                wait("vector", "sQ", mk_t)
                for (qs, qw_) in _suffix_chunks(g):
                    first_qc = (qs == 256 * g)
                    for hq in range(4):
                        exps = []
                        for hi in range(4):
                            h = 4 * hq + hi
                            r0 = (h % 2) * 64
                            tile = h // 2
                            for kt in range(4):
                                bank = (4 * hi + kt) % 4
                                if len(exps) >= 4:
                                    wait("tensor", "sA", exps[-4])
                                p_t = pe(lambda t, r0=r0, tile=tile, kt=kt, bank=bank, qs=qs, qw_=qw_:
                                         t.matmul(ps[bank][:, 0:qw_],
                                                  kbl[tile][r0:r0 + 64, kt * 128:(kt + 1) * 128],
                                                  bigA[tile][r0:r0 + 64, qs:qs + qw_],
                                                  start=True, stop=True), bump=True)
                                wait("scalar", "sP", p_t)
                                a_t = act(lambda s_, hi=hi, kt=kt, bank=bank, qw_=qw_:
                                          s_.activation(ptl[4 * hi + kt][:, 0:qw_],
                                                        ps[bank][:, 0:qw_],
                                                        AF.Exp, scale=float(1.0 / 32.0)),
                                          bump=True)
                                exps.append(a_t)
                        gate = ("sA", exps[-1])
                        if first_qc:
                            wait("vector", "sA", exps[-1])
                            m_t = 0
                            for hi in range(4):
                                for kt in range(4):
                                    for qt in range(2):
                                        m_t = dve(lambda v, hi=hi, kt=kt, qt=qt: v.tensor_mul(
                                            ptl[4 * hi + kt][:, qt * 128:(qt + 1) * 128],
                                            ptl[4 * hi + kt][:, qt * 128:(qt + 1) * 128],
                                            mbuf[:, (2 * kt + qt) * 128:(2 * kt + qt + 1) * 128]),
                                            bump=True)
                            gate = ("sV", m_t)
                        pe_drain()
                        for hi in range(4):
                            h = 4 * hq + hi
                            bank = 4 + hi
                            if first_qc:
                                wait("tensor", gate[0], gate[1])
                            else:
                                wait("tensor", "sA", exps[4 * hi + 3])
                            p_t = 0
                            for kt in range(4):
                                p_t = pe(lambda t, h=h, hi=hi, kt=kt, bank=bank, qw_=qw_:
                                         t.matmul(ps[bank][0:65, 0:qw_],
                                                  vbl[kt][:, h * 65:h * 65 + 65],
                                                  ptl[4 * hi + kt][:, 0:qw_],
                                                  start=(kt == 0), stop=(kt == 3)),
                                         bump=(kt == 3))
                            wait("vector", "sP", p_t)
                            dve(lambda v, h=h, bank=bank, qs=qs, qw_=qw_: v.tensor_tensor(
                                bigB[h // 2][(h % 2) * 64:(h % 2) * 64 + 64, qs:qs + qw_],
                                bigB[h // 2][(h % 2) * 64:(h % 2) * 64 + 64, qs:qs + qw_],
                                ps[bank][0:64, 0:qw_], ALU.add), bump=True)
                            slot = len(stage_hist) % 2
                            if len(stage_hist) >= 2:
                                wait("vector", "sG", stage_hist[-2])
                            v_t = dve(lambda v, bank=bank, slot=slot, qw_=qw_: v.tensor_copy(
                                stage[64:65, slot * 512:slot * 512 + qw_],
                                ps[bank][64:65, 0:qw_]), bump=True)
                            wait("gpsimd", "sV", v_t)
                            g_t = dma("gpsimd", dens[h, g, qs:qs + qw_],
                                      bass.AP(stage, 64 * 1024 + slot * 512, [[1024, 1], [1, qw_]]))
                            stage_hist.append(g_t)
                        wait("tensor", "sV", cnt["sV"])
                        pe_drain()
            barrier()
        s3.close()

        s4 = ExitStack()
        st["rows"] = mk(s4, "rows4", [1, 3 * 512])
        st["mub"] = mk(s4, "mub4", [128, 512])
        st["rb"] = mk(s4, "rb4", [128, 512])
        den = mk(s4, "den", [16, TOWN])
        bcst = mk(s4, "bcst", [128, TOWN])
        wob = mk(s4, "wob", [128, 2 * 1024])
        xrt = kbl
        sqt = [vbl[i % 4][:, (i // 4) * 512:(i // 4) * 512 + 512] for i in range(DCH)]
        dtmp = bcst
        if _STAGE >= 4:
            d_t = dma("sync", den[:, :], bass.AP(dens.tensor, 0, [[NG * TOWN, 16], [1, TOWN]]))
            wait("vector", "sQ", d_t)
            for g in range(1, NG):
                w_g = TOWN - 256 * g
                wait("sync", "sV", cnt["sV"])
                d_t = dma("sync", dtmp[0:16, 0:w_g],
                          bass.AP(dens.tensor, g * TOWN + 256 * g, [[NG * TOWN, 16], [1, w_g]]))
                wait("vector", "sQ", d_t)
                dve(lambda v, g=g, w_g=w_g: v.tensor_tensor(
                    den[0:16, 256 * g:TOWN], den[0:16, 256 * g:TOWN],
                    dtmp[0:16, 0:w_g], ALU.add), bump=True)
            v_t = dve(lambda v: v.reciprocal(den[:, :], den[:, :]), bump=True)
            if _TAPS:
                wait("gpsimd", "sV", v_t)
                g_t = 0
                for i in range(DCH):
                    g_t = dma("gpsimd", taps["tap_num"][i * 128:(i + 1) * 128, :], bigB[i][:])
                g_t = dma("gpsimd", taps["tap_den"], den[:, :])
                wait("gpsimd", "sG", g_t)
            wait("gpsimd", "sV", v_t)
            for j in range(DCH):
                wait("gpsimd", "sV", cnt["sV"])
                dma("gpsimd", bcst[0:64, :],
                    bass.AP(den, (2 * j) * TOWN, [[TOWN, 1], [0, 64], [1, TOWN]]))
                b_t = dma("gpsimd", bcst[64:128, :],
                          bass.AP(den, (2 * j + 1) * TOWN, [[TOWN, 1], [0, 64], [1, TOWN]]))
                wait("vector", "sG", b_t)
                dve(lambda v, j=j: v.tensor_mul(bigB[j][:], bigB[j][:], bcst[:]), bump=True)
            att_v = cnt["sV"]
            if _TAPS:
                wait("gpsimd", "sV", att_v)
                g_t = 0
                for i in range(DCH):
                    g_t = dma("gpsimd", taps["tap_att"][i * 128:(i + 1) * 128, :], bigB[i][:])
                wait("gpsimd", "sG", g_t)
            wait("tensor", "sV", att_v)
            for lc in range(NLCH):
                wait("sync", "sG", cnt["sG"])
                wait("sync", "sP", cnt["sP"])
                wait("sync", "sV", cnt["sV"])
                xw = 0
                for i in range(DCH):
                    xw = dma("sync", xrt[i][:],
                             bass.AP(xt.tensor, i * 128 * T + lc * 1024,
                                     [[T, 128], [512, 2], [1, 256]]))
                ph = []
                for o in range(DCH):
                    sl = (o % 2) * 1024
                    if len(ph) >= 2:
                        wait("sync", "sP", ph[-2])
                    lw = dma("sync", wob[:, sl:sl + 1024],
                             bass.AP(wo.tensor, o * 128, [[D, 128], [128 * D, 8], [1, 128]]))
                    wait("tensor", "sQ", lw)
                    if o >= 2:
                        wait("tensor", "sV", cnt["sV"])
                    p_t = 0
                    for i in range(DCH):
                        p_t = pe(lambda t, i=i, o=o, sl=sl, lc=lc: t.matmul(
                            ps[o % 2][:, 0:512], wob[:, sl + i * 128:sl + (i + 1) * 128],
                            bigB[i][:, lc * 512:(lc + 1) * 512],
                            start=(i == 0), stop=(i == DCH - 1)), bump=(i == DCH - 1))
                    ph.append(p_t)
                    if o == 0:
                        wait("vector", "sQ", xw)
                    wait("vector", "sP", p_t)
                    dve(lambda v, o=o: v.tensor_scalar(
                        sqt[o], ps[o % 2][:, 0:512],
                        cb[:, 3 * DCH + o:3 * DCH + o + 1], None, ALU.add))
                    v_t = dve(lambda v, o=o: v.tensor_tensor(
                        xrt[o][:], sqt[o], xrt[o][:], ALU.add), bump=True)
                    wait("gpsimd", "sV", v_t)
                    dma("gpsimd", xr2s[o * 128:(o + 1) * 128, lc * 512:(lc + 1) * 512], xrt[o][:])
                if not _NOLN4:
                    g_t = ln_stats(lambda i: xrt[i][:], lambda i: sqt[i], 512)
                    wait("vector", "sG", g_t)
                    for i in range(DCH):
                        dve(lambda v, i=i: v.tensor_sub(xrt[i][:], xrt[i][:], st["mub"][:, 0:512]))
                        dve(lambda v, i=i, lc=lc: v.tensor_mul(
                            bigA[i][:, lc * 512:(lc + 1) * 512], xrt[i][:], st["rb"][:, 0:512]),
                            bump=True)
            barrier()
            if _TAPS:
                g_t = 0
                for i in range(DCH):
                    g_t = dma("gpsimd", taps["tap_h0"][i * 128:(i + 1) * 128, :], bigA[i][:])
                g_t = dma("gpsimd", taps["tap_xr2"], xr2s)
                wait("gpsimd", "sG", g_t)
        s4.close()

        s5 = ExitStack()
        st["rows"] = mk(s5, "rows5", [1, 3 * 512])
        st["mub"] = mk(s5, "mub5", [128, 512])
        st["rb"] = mk(s5, "rb5", [128, 512])
        wff = [mk(s5, f"wff{i}", [128, 1024]) for i in range(DCH)]
        qwchS = [vbl[0][:, 0:1024].bitcast(I32), vbl[1][:, 0:1024].bitcast(I32)]
        scchS = [vbl[2][:, 0:1024], vbl[3][:, 0:1024]]
        y1 = kbl
        lasbT = mk(s5, "lasbT", [128, DCH * R])
        lbsbT = mk(s5, "lbsbT", [R, DCH * 128])
        lasb = [lasbT[:, i * R:(i + 1) * R] for i in range(DCH)]
        lbsb = [lbsbT[0:R, o * 128:(o + 1) * 128] for o in range(DCH)]
        t1sb = vbl[0][0:R, 0:512]

        def qlora(midx, in_tiles_fn, out_cb):
            wait("sync", "sP", cnt["sP"])
            wait("sync", "sV", cnt["sV"])
            deq_hist = []
            for o in range(DCH):
                sl = o % 2
                wait("sync", "sV", cnt["sV"])
                lw = dma("sync", qwchS[sl],
                         bass.AP(qw.tensor, midx * D * D + o * 128,
                                 [[D, 128], [128 * D, 8], [1, 128]]))
                lw = dma("sync", scchS[sl],
                         bass.AP(sce.tensor, midx * D * D + o * 128,
                                 [[D, 128], [128 * D, 8], [1, 128]]))
                wait("vector", "sQ", lw)
                if o == 0:
                    wait("vector", "sP", cnt["sP"])
                v_t = 0
                for i in range(DCH):
                    v_t = dve(lambda v, i=i, o=o, sl=sl: v.scalar_tensor_tensor(
                        wff[i][:, o * 128:(o + 1) * 128],
                        qwchS[sl][:, i * 128:(i + 1) * 128], 7.5,
                        scchS[sl][:, i * 128:(i + 1) * 128],
                        ALU.subtract, ALU.mult), bump=True)
                deq_hist.append(v_t)
            wait("sync", "sV", cnt["sV"])
            lw = 0
            for i in range(DCH):
                lw = dma("sync", lasb[i],
                         bass.AP(lat.tensor, midx * D * R + i * 128 * R, [[R, 128], [1, R]]))
            for o in range(DCH):
                lw = dma("sync", lbsb[o],
                         bass.AP(lbt.tensor, midx * R * D + o * 128, [[D, R], [1, 128]]))
            wait("tensor", "sQ", lw)
            wait("tensor", "sV", cnt["sV"])
            if _TAPS and midx == int(_os.environ.get('TAPMIDX', '0')):
                wait("gpsimd", "sV", cnt["sV"])
                for i in range(DCH):
                    g_t = dma("gpsimd", taps["tap_wff"][i * 128:(i + 1) * 128, :], wff[i][:])
                wait("gpsimd", "sG", g_t)
            for lc in range(NLCH):
                wait("tensor", "sV", cnt["sV"])
                tiles = in_tiles_fn(lc)
                p_t = 0
                for i in range(DCH):
                    p_t = pe(lambda t, i=i, ap_=tiles[i]: t.matmul(
                        ps[6][0:R, 0:512], lasb[i], ap_,
                        start=(i == 0), stop=(i == DCH - 1)), bump=(i == DCH - 1))
                wait("vector", "sP", p_t)
                v_t = dve(lambda v: v.tensor_scalar(t1sb, ps[6][0:R, 0:512], 7.5, None, ALU.mult),
                          bump=True)
                if _TAPS and midx == int(_os.environ.get('TAPMIDX', '0')) and lc == 0:
                    wait("gpsimd", "sV", v_t)
                    g_t2 = dma("gpsimd", taps["tap_t1"], t1sb)
                    wait("gpsimd", "sG", g_t2)
                for o in range(DCH):
                    if o == 6:
                        wait("tensor", "sV", v_t)
                    p_t = 0
                    for i in range(DCH):
                        p_t = pe(lambda t, i=i, o=o, ap_=tiles[i]: t.matmul(
                            ps[o][:, 0:512], wff[i][:, o * 128:(o + 1) * 128], ap_,
                            start=(i == 0), stop=(i == DCH - 1)),
                            bump=(i == DCH - 1 and o == DCH - 1))
                pe_drain()
                wait("tensor", "sV", v_t)
                p_t = 0
                for o in range(DCH):
                    p_t = pe(lambda t, o=o: t.matmul(
                        ps[o][:, 0:512], lbsb[o], t1sb,
                        start=False, stop=True), bump=(o == DCH - 1))
                pe_drain()
                wait("vector", "sP", p_t)
                out_cb(lc, p_t)

        for blk in range(3):
            def in_h(lc):
                return [bigA[i][:, lc * 512:(lc + 1) * 512] for i in range(DCH)]

            def out1(lc, p_t, blk=blk):
                for o in range(DCH):
                    dve(lambda v, o=o, blk=blk: v.tensor_scalar(
                        y1[o][:], ps[o][:, 0:512], float(2.0 / 15.0),
                        cfb[:, (2 * blk) * DCH + o:(2 * blk) * DCH + o + 1],
                        ALU.mult, ALU.add), bump=(o == DCH - 1))
                g_t = ln_stats(lambda i: y1[i][:],
                               lambda i, lc=lc: bigB[i][:, lc * 512:(lc + 1) * 512], 512)
                wait("vector", "sG", g_t)
                v_t = 0
                for i in range(DCH):
                    dve(lambda v, i=i: v.tensor_sub(y1[i][:], y1[i][:], st["mub"][:, 0:512]))
                    v_t = dve(lambda v, i=i: v.tensor_mul(y1[i][:], y1[i][:], st["rb"][:, 0:512]),
                              bump=(i == DCH - 1))
                wait("scalar", "sV", v_t)
                a_t = 0
                for i in range(DCH):
                    a_t = act(lambda s_, i=i, lc=lc, blk=blk: s_.activation(
                        bigB[i][:, lc * 512:(lc + 1) * 512], y1[i][:], AF.Gelu,
                        bias=cb3[:, blk * DCH + i:blk * DCH + i + 1],
                        scale=cg3[:, blk * DCH + i:blk * DCH + i + 1]), bump=(i == DCH - 1))
                wait("tensor", "sA", a_t)

            def in_y2(lc):
                return [bigB[i][:, lc * 512:(lc + 1) * 512] for i in range(DCH)]

            def out2(lc, p_t, blk=blk):
                v_t = 0
                for o in range(DCH):
                    dve(lambda v, o=o, blk=blk: v.tensor_scalar(
                        y1[o][:], ps[o][:, 0:512], float(2.0 / 15.0),
                        cfb[:, (2 * blk + 1) * DCH + o:(2 * blk + 1) * DCH + o + 1],
                        ALU.mult, ALU.add))
                    v_t = dve(lambda v, o=o, lc=lc: v.tensor_tensor(
                        bigA[o][:, lc * 512:(lc + 1) * 512],
                        bigA[o][:, lc * 512:(lc + 1) * 512], y1[o][:], ALU.add), bump=True)
                wait("tensor", "sV", v_t)

            if _STAGE >= 5:
                qlora(2 * blk, in_h, out1)
                if _TAPS and blk == int(_os.environ.get('TAPBLK', '-1')):
                    wait("gpsimd", "sA", cnt["sA"])
                    wait("gpsimd", "sV", cnt["sV"])
                    for i in range(DCH):
                        g_t = dma("gpsimd", taps["tap_y2"][i * 128:(i + 1) * 128, :], bigB[i][:])
                    wait("gpsimd", "sG", g_t)
                qlora(2 * blk + 1, in_y2, out2)
                if _TAPS and blk == int(_os.environ.get('TAPBLK', '-1')):
                    wait("gpsimd", "sV", cnt["sV"])
                    for i in range(DCH):
                        g_t = dma("gpsimd", taps["tap_h0"][i * 128:(i + 1) * 128, :], bigA[i][:])
                    wait("gpsimd", "sG", g_t)
                barrier()
        if _TAPS and _STAGE >= 5:
            g_t = 0
            for i in range(DCH):
                g_t = dma("gpsimd", taps["tap_y2"][i * 128:(i + 1) * 128, :], bigA[i][:])
            wait("gpsimd", "sG", g_t)

        st_t = 0
        if _STAGE >= 5:
            for lc in range(NLCH):
                wait("sync", "sG", cnt["sG"])
                xw = 0
                for i in range(DCH):
                    xw = dma("sync", y1[i][:],
                             xr2s[i * 128:(i + 1) * 128, lc * 512:(lc + 1) * 512])
                wait("vector", "sQ", xw)
                v_t = 0
                for i in range(DCH):
                    v_t = dve(lambda v, i=i, lc=lc: v.tensor_tensor(
                        y1[i][:], y1[i][:], bigA[i][:, lc * 512:(lc + 1) * 512], ALU.add),
                        bump=True)
                wait("gpsimd", "sV", v_t)
                for i in range(DCH):
                    st_t = dma("gpsimd", out_t[i * 128:(i + 1) * 128, lc * 512:(lc + 1) * 512],
                               y1[i][:])
            wait("gpsimd", "sG", st_t)
        else:
            z_t = dve(lambda v: v.memset(bigB[0][:, 0:512], 0.0), bump=True)
            wait("gpsimd", "sV", z_t)
            for i in range(DCH):
                for lc in range(NLCH):
                    st_t = dma("gpsimd", out_t[i * 128:(i + 1) * 128, lc * 512:(lc + 1) * 512],
                               bigB[0][:, 0:512])
            wait("gpsimd", "sG", st_t)
        s5.close()
        s13.close()

        @block.sync
        def _(e):
            for f in plan["sync"]:
                f(e)

        @block.gpsimd
        def _(e):
            for f in plan["gpsimd"]:
                f(e)

        @block.tensor
        def _(e):
            for f in plan["tensor"]:
                f(e)

        @block.vector
        def _(e):
            for f in plan["vector"]:
                f(e)

        @block.scalar
        def _(e):
            for f in plan["scalar"]:
                f(e)

    build_nc._cnt = dict(cnt)
    return nc


def _prep_inputs(inputs):
    x = np.asarray(inputs["L0_attn_x"], dtype=np.float32)
    wqw = np.asarray(inputs["L0_attn_Wq_w"], dtype=np.float32)
    wkw = np.asarray(inputs["L0_attn_Wk_w"], dtype=np.float32)
    wvw = np.asarray(inputs["L0_attn_Wv_w"], dtype=np.float32)
    wow = np.asarray(inputs["L0_attn_Wo_w"], dtype=np.float32)
    bq = np.asarray(inputs["L0_attn_Wq_b"], dtype=np.float32)
    bk = np.asarray(inputs["L0_attn_Wk_b"], dtype=np.float32)
    bv = np.asarray(inputs["L0_attn_Wv_b"], dtype=np.float32)
    bo = np.asarray(inputs["L0_attn_Wo_b"], dtype=np.float32)
    qwq = np.asarray(inputs["L0_ffn_qweights"], dtype=np.int32)
    scl = np.asarray(inputs["L0_ffn_scales"], dtype=np.float32)
    fbi = np.asarray(inputs["L0_ffn_biases"], dtype=np.float32)
    la = np.asarray(inputs["L0_ffn_lora_a"], dtype=np.float32)
    lb = np.asarray(inputs["L0_ffn_lora_b"], dtype=np.float32)
    lg = np.asarray(inputs["L0_ffn_ln_gamma"], dtype=np.float32)
    lbe = np.asarray(inputs["L0_ffn_ln_beta"], dtype=np.float32)

    wq_t = np.ascontiguousarray(wqw.T)
    wk_t = np.ascontiguousarray(wkw.T)
    wv_t = np.ascontiguousarray(wvw.T)
    wo_t = np.ascontiguousarray(wow.T)
    bqkvo = np.ascontiguousarray(np.concatenate([bq, bk, bv, bo]).reshape(4 * DCH, 128).T)
    bvb = np.ascontiguousarray(np.broadcast_to(bv[None, :], (128, D)))
    qw_t = np.ascontiguousarray(qwq.transpose(0, 2, 1))
    sce = np.ascontiguousarray(np.repeat(scl, 16, axis=2).transpose(0, 2, 1)) if scl.shape[2] == 64 else None
    sce = np.ascontiguousarray(np.repeat(scl, 16, axis=2).transpose(0, 2, 1))
    fb_t = np.ascontiguousarray(fbi.reshape(6 * DCH, 128).T)
    lat = np.ascontiguousarray(la.transpose(0, 2, 1))
    lbt = np.ascontiguousarray(lb.transpose(0, 2, 1))
    g3 = np.ascontiguousarray(lg.reshape(3 * DCH, 128).T)
    be3 = np.ascontiguousarray(lbe.reshape(3 * DCH, 128).T)
    consts = np.zeros((128, 2), dtype=np.float32)
    consts[:, 0] = EPS
    consts[:, 1] = 1.0 / D

    import ml_dtypes
    in_maps = []
    for c in range(8):
        b, p = c // 2, c % 2
        perm = np.empty(T, dtype=np.int64)
        for m in range(NCH):
            perm[m * 512:m * 512 + 256] = np.arange((2 * m + p) * 256, (2 * m + p) * 256 + 256)
            perm[m * 512 + 256:(m + 1) * 512] = np.arange((2 * m + 1 - p) * 256,
                                                          (2 * m + 1 - p) * 256 + 256)
        xt_c = np.ascontiguousarray(x[b].T[:, perm])
        mk_ = np.zeros((NG, 128, 1024), dtype=np.float32)
        for g in range(NG):
            kcols = perm[g * 512:(g + 1) * 512]
            qpos = (2 * g + p) * 256 + np.arange(256)
            for kt in range(4):
                for qt in range(2):
                    kk = kcols[kt * 128:(kt + 1) * 128]
                    qq = qpos[qt * 128:(qt + 1) * 128]
                    mk_[g, :, (2 * kt + qt) * 128:(2 * kt + qt + 1) * 128] = (
                        qq[None, :] >= kk[:, None]).astype(np.float32)
        in_maps.append(dict(
            xt=xt_c, wq=wq_t, wk=wk_t, wv=wv_t, wo=wo_t, bqkvo=bqkvo, bvb=bvb,
            qw=qw_t, sce=sce, fb=fb_t, lat=lat, lbt=lbt, g3=g3, be3=be3,
            masks=mk_.astype(ml_dtypes.bfloat16), consts=consts))
    return in_maps


def _gather(results):
    out = np.empty((B, T, D), dtype=np.float32)
    for c in range(8):
        b, p = c // 2, c % 2
        ot = np.asarray(results[c]["out_t"])
        own = np.empty(TOWN, dtype=np.int64)
        for m in range(NCH):
            own[m * 256:(m + 1) * 256] = np.arange((2 * m + p) * 256, (2 * m + p) * 256 + 256)
        out[b, own, :] = ot.T
    return out


def kernel(**inputs):
    in_maps = _prep_inputs(inputs)
    nc = build_nc()
    res = run_bass_kernel_spmd(nc, in_maps, core_ids=list(range(8)))
    return _gather(res.results)


# version 08
# speedup vs baseline: 1.0615x; 1.0250x over previous
import os as _os
import numpy as np
import concourse.bass as bass
import concourse.mybir as mybir
from concourse.bass_utils import run_bass_kernel_spmd
from contextlib import ExitStack

F32 = mybir.dt.float32
BF16 = mybir.dt.bfloat16
I32 = mybir.dt.int32
AF = mybir.ActivationFunctionType
ALU = mybir.AluOpType

B, T, D, H, HD = 4, 4096, 1024, 16, 64
TOWN = 2048
NCH = 8
NLCH = 4
DCH = 8
EPS = 1e-5
R = 32
VW = H * (HD + 1)
NG = 8

_STAGE = 99
_TAPS = False
_NOLN4 = False


def _suffix_chunks(g):
    s, out = 256 * g, []
    while s < TOWN:
        w = min(512, TOWN - s)
        out.append((s, w))
        s += w
    return out


def build_nc():
    nc = bass.Bass("TRN2", target_bir_lowering=False, detect_race_conditions=False)
    xt = nc.dram_tensor("xt", [D, T], F32, kind="ExternalInput").ap()
    wq = nc.dram_tensor("wq", [D, D], F32, kind="ExternalInput").ap()
    wk = nc.dram_tensor("wk", [D, D], F32, kind="ExternalInput").ap()
    wv = nc.dram_tensor("wv", [D, D], F32, kind="ExternalInput").ap()
    wo = nc.dram_tensor("wo", [D, D], F32, kind="ExternalInput").ap()
    bqkvo = nc.dram_tensor("bqkvo", [128, 4 * DCH], F32, kind="ExternalInput").ap()
    bvb = nc.dram_tensor("bvb", [128, D], F32, kind="ExternalInput").ap()
    qw = nc.dram_tensor("qw", [6, D, D], I32, kind="ExternalInput").ap()
    sce = nc.dram_tensor("sce", [6, D, D], F32, kind="ExternalInput").ap()
    fb = nc.dram_tensor("fb", [128, 6 * DCH], F32, kind="ExternalInput").ap()
    lat = nc.dram_tensor("lat", [6, D, R], F32, kind="ExternalInput").ap()
    lbt = nc.dram_tensor("lbt", [6, R, D], F32, kind="ExternalInput").ap()
    g3 = nc.dram_tensor("g3", [128, 3 * DCH], F32, kind="ExternalInput").ap()
    be3 = nc.dram_tensor("be3", [128, 3 * DCH], F32, kind="ExternalInput").ap()
    masks = nc.dram_tensor("masks", [NG, 128, 1024], BF16, kind="ExternalInput").ap()
    consts = nc.dram_tensor("consts", [128, 2], F32, kind="ExternalInput").ap()
    out_t = nc.dram_tensor("out_t", [D, TOWN], F32, kind="ExternalOutput").ap()
    kvs = nc.dram_tensor("kvs", [D, T], F32, kind="Internal").ap()
    vvs = nc.dram_tensor("vvs", [T, VW], F32, kind="Internal").ap()
    xr2s = nc.dram_tensor("xr2s", [D, TOWN], F32, kind="Internal").ap()
    dens = nc.dram_tensor("dens", [H, NG, TOWN], F32, kind="Internal").ap()
    taps = {}
    if _TAPS:
        for nm, shp in (("tap_q", [D, TOWN]), ("tap_k", [D, T]), ("tap_v", [T, VW]),
                        ("tap_num", [D, TOWN]), ("tap_den", [16, TOWN]),
                        ("tap_att", [D, TOWN]), ("tap_h0", [D, TOWN]),
                        ("tap_y2", [D, TOWN]), ("tap_xr2", [D, TOWN]),
                        ("tap_wff", [D, D]), ("tap_t1", [R, 512])):
            taps[nm] = nc.dram_tensor(nm, shp, F32, kind="ExternalOutput").ap()

    top = ExitStack()
    with top:
        def mk(stk, n_, s_, d_=F32):
            return stk.enter_context(nc.sbuf_tensor(n_, s_, d_))
        bigB = [mk(top, f"bigB{i}", [128, TOWN]) for i in range(DCH)]
        st = {}
        cb = mk(top, "cb", [128, 4 * DCH])
        cfb = mk(top, "cfb", [128, 6 * DCH])
        cg3 = mk(top, "cg3", [128, 3 * DCH])
        cb3 = mk(top, "cb3", [128, 3 * DCH])
        ccon = mk(top, "ccon", [128, 2])
        ps = [top.enter_context(nc.psum_tensor(f"ps{i}", [128, 512], F32)) for i in range(8)]

        sQ = top.enter_context(nc.semaphore("sQ"))
        sG = top.enter_context(nc.semaphore("sG"))
        sP = top.enter_context(nc.semaphore("sP"))
        sA = top.enter_context(nc.semaphore("sA"))
        sV = top.enter_context(nc.semaphore("sV"))
        sB_ = top.enter_context(nc.semaphore("sB"))
        block = top.enter_context(nc.Block())

        plan = {e: [] for e in ("sync", "gpsimd", "tensor", "vector", "scalar")}
        cnt = {"sQ": 0, "sG": 0, "sP": 0, "sA": 0, "sV": 0, "bar": 0}
        sem = {"sQ": sQ, "sG": sG, "sP": sP, "sA": sA, "sV": sV, "bar": sB_}
        ENGS = ("sync", "gpsimd", "tensor", "vector", "scalar")

        def emit(eng, fn):
            plan[eng].append(fn)

        def dma(eng_name, out, in_):
            sname = "sQ" if eng_name == "sync" else "sG"
            cnt[sname] += 16
            s = sem[sname]
            emit(eng_name, lambda e, o=out, i=in_: e.dma_start(out=o, in_=i).then_inc(s, 16))
            return cnt[sname]

        def wait(eng_name, sem_name, val):
            if val <= 0:
                return
            s = sem[sem_name]
            emit(eng_name, lambda e, v=val: e.wait_ge(s, v))

        def dve(fn, bump=False):
            if bump:
                cnt["sV"] += 1
                emit("vector", lambda v, f=fn: f(v).then_inc(sV, 1))
                return cnt["sV"]
            emit("vector", fn)

        def act(fn, bump=False):
            if bump:
                cnt["sA"] += 1
                emit("scalar", lambda s_, f=fn: f(s_).then_inc(sA, 1))
                return cnt["sA"]
            emit("scalar", fn)

        def pe(fn, bump=False):
            if bump:
                cnt["sP"] += 1
                emit("tensor", lambda t, f=fn: f(t).then_inc(sP, 1))
                return cnt["sP"]
            emit("tensor", fn)

        def pe_drain():
            emit("tensor", lambda t: t.drain())

        def barrier():
            cnt["bar"] += len(ENGS)
            tgt = cnt["bar"]
            for e in ENGS:
                if e == "tensor":
                    emit(e, lambda t: t.drain().then_inc(sB_, 1))
                else:
                    emit(e, lambda t: t.nop().then_inc(sB_, 1))
            for e in ENGS:
                wait(e, "bar", tgt)

        stats_mu = [0]
        stats_r1 = [0]

        def row(i, w=512):
            return st["rows"][0:1, i * 512:i * 512 + w]

        MU, SS, R1 = range(3)
        R0 = SS
        TMP = R1

        def ln_stats(src_fn, sq_fn, width):
            wait("scalar", "sV", cnt["sV"])
            a_t = 0
            for i in range(DCH):
                a_t = act(lambda s_, i=i: s_.activation(sq_fn(i), src_fn(i), AF.Square),
                          bump=True)
            wait("tensor", "sV", cnt["sV"])
            wait("tensor", "sA", a_t)
            for i in range(DCH):
                pe(lambda t, i=i: t.matmul(ps[6][0:1, 0:width], ccon[:, 1:2], src_fn(i),
                                           start=(i == 0), stop=(i == DCH - 1)))
            p_t = 0
            for i in range(DCH):
                p_t = pe(lambda t, i=i: t.matmul(ps[7][0:1, 0:width], ccon[:, 1:2], sq_fn(i),
                                                 start=(i == 0), stop=(i == DCH - 1)),
                         bump=(i == DCH - 1))
            wait("vector", "sP", p_t)
            stats_mu[0] = dve(lambda v: v.tensor_copy(row(MU, width), ps[6][0:1, 0:width]),
                              bump=True)
            dve(lambda v: v.tensor_mul(row(TMP, width), row(MU, width), row(MU, width)))
            v_t = dve(lambda v: v.tensor_sub(ps[7][0:1, 0:width], ps[7][0:1, 0:width],
                                             row(TMP, width)), bump=True)
            wait("scalar", "sV", v_t)
            a_t = act(lambda s_: s_.activation(row(SS, width), ps[7][0:1, 0:width], AF.Sqrt,
                                               bias=ccon[0:1, 0:1]), bump=True)
            wait("vector", "sA", a_t)
            dve(lambda v: v.reciprocal(row(R0, width), row(SS, width)))
            dve(lambda v: v.tensor_scalar(ps[7][0:1, 0:width], ps[7][0:1, 0:width],
                                          float(EPS), None, ALU.add))
            dve(lambda v: v.tensor_mul(row(R1, width), row(R0, width), row(R0, width)))
            dve(lambda v: v.tensor_mul(row(R1, width), row(R1, width), ps[7][0:1, 0:width]))
            dve(lambda v: v.tensor_scalar(row(R1, width), row(R1, width), -0.5, 1.5, ALU.mult, ALU.add))
            v_t = dve(lambda v: v.tensor_mul(row(R1, width), row(R1, width), row(R0, width)), bump=True)
            stats_r1[0] = v_t
            wait("gpsimd", "sV", v_t)
            dma("gpsimd", st["mub"][:, 0:width],
                bass.AP(st["rows"], MU * 512, [[3 * 512, 1], [0, 128], [1, width]]))
            g_t = dma("gpsimd", st["rb"][:, 0:width],
                      bass.AP(st["rows"], R1 * 512, [[3 * 512, 1], [0, 128], [1, width]]))
            return g_t

        dma("sync", cb[:], bqkvo)
        dma("sync", cfb[:], fb)
        dma("sync", cg3[:], g3)
        dma("sync", cb3[:], be3)
        c0 = dma("sync", ccon[:], consts)

        s13 = ExitStack()
        kbl = [mk(s13, f"kbl{i}", [128, 512]) for i in range(DCH)]
        vbl = [mk(s13, f"vbl{i}", [128, VW]) for i in range(4)]
        bigA = [mk(s13, f"bigA{i}", [128, TOWN]) for i in range(DCH)]
        s1 = ExitStack()
        st["rows"] = mk(s1, "rows1", [1, 3 * 512])
        st["mub"] = mk(s1, "mub1", [128, 512])
        st["rb"] = mk(s1, "rb1", [128, 512])
        xch = [mk(s1, f"xch{i}", [128, 512]) for i in range(DCH)]
        wbuf = mk(s1, "wbuf", [128, 2 * 1024])
        wvb = kbl
        cbvb = mk(s1, "cbvb", [128, D])
        cv0 = dma("sync", cbvb[:], bvb)

        if _STAGE >= 1:
            wait("vector", "sQ", c0)
            wait("scalar", "sQ", c0)
            for m in range(NCH):
                wait("sync", "sP", cnt["sP"])
                xw = 0
                for i in range(DCH):
                    xw = dma("sync", xch[i][:], xt[i * 128:(i + 1) * 128, m * 512:(m + 1) * 512])
                wait("scalar", "sQ", xw)
                wait("vector", "sQ", xw)
                wait("scalar", "sG", cnt["sG"])
                g_t = ln_stats(lambda i: xch[i][:], lambda i: kbl[i][:], 512)
                wait("vector", "sG", g_t)
                v_t = 0
                for i in range(DCH):
                    dve(lambda v, i=i: v.tensor_sub(xch[i][:], xch[i][:], st["mub"][:, 0:512]))
                    v_t = dve(lambda v, i=i: v.tensor_mul(xch[i][:], xch[i][:], st["rb"][:, 0:512]),
                              bump=True)
                wait("tensor", "sV", v_t)
                ph = []
                ev = []
                for o in range(DCH):
                    sl = (o % 2) * 1024
                    if len(ph) >= 2:
                        wait("sync", "sP", ph[-2])
                    lw = dma("sync", wbuf[:, sl:sl + 1024],
                             bass.AP(wq.tensor, o * 128, [[D, 128], [128 * D, 8], [1, 128]]))
                    wait("tensor", "sQ", lw)
                    if len(ev) >= 2:
                        wait("tensor", "sV", ev[-2])
                    p_t = 0
                    for i in range(DCH):
                        p_t = pe(lambda t, i=i, o=o, sl=sl: t.matmul(
                            ps[o % 2][:, 0:256], wbuf[:, sl + i * 128:sl + (i + 1) * 128],
                            xch[i][:, 0:256], start=(i == 0), stop=(i == DCH - 1)),
                            bump=(i == DCH - 1))
                    ph.append(p_t)
                    wait("vector", "sP", p_t)
                    ev.append(dve(lambda v, o=o, m=m: v.tensor_scalar(
                        bigA[o][:, m * 256:(m + 1) * 256], ps[o % 2][:, 0:256],
                        cb[:, o:o + 1], None, ALU.add), bump=True))
                ev = []
                for o in range(DCH):
                    sl = (o % 2) * 1024
                    if len(ph) >= 2:
                        wait("sync", "sP", ph[-2])
                    lw = dma("sync", wbuf[:, sl:sl + 1024],
                             bass.AP(wk.tensor, o * 128, [[D, 128], [128 * D, 8], [1, 128]]))
                    wait("tensor", "sQ", lw)
                    if len(ev) >= 2:
                        wait("tensor", "sV", ev[-2])
                    elif o == 0:
                        wait("tensor", "sV", cnt["sV"])
                    p_t = 0
                    for i in range(DCH):
                        p_t = pe(lambda t, i=i, o=o, sl=sl: t.matmul(
                            ps[2 + o % 2][:, 0:512], wbuf[:, sl + i * 128:sl + (i + 1) * 128],
                            xch[i][:, 0:512], start=(i == 0), stop=(i == DCH - 1)),
                            bump=(i == DCH - 1))
                    ph.append(p_t)
                    wait("vector", "sP", p_t)
                    wait("vector", "sG", cnt["sG"] if o == 0 else 0)
                    v_t = dve(lambda v, o=o: v.tensor_scalar(
                        kbl[o][:, 0:512], ps[2 + o % 2][:, 0:512],
                        cb[:, DCH + o:DCH + o + 1], None, ALU.add), bump=True)
                    ev.append(v_t)
                    wait("gpsimd", "sV", v_t)
                    dma("gpsimd", kvs[o * 128:(o + 1) * 128, m * 512:(m + 1) * 512],
                        kbl[o][:, 0:512])
                wait("vector", "sQ", cv0)
                wait("vector", "sG", cnt["sG"])
                z_t = 0
                for tt in range(4):
                    z_t = dve(lambda v, tt=tt: v.memset(
                        bass.AP(vbl[tt], 64, [[VW, 128], [65, 16], [1, 1]]), 1.0), bump=True)
                for hh in range(2):
                    wait("sync", "sP", cnt["sP"])
                    wait("sync", "sG", cnt["sG"])
                    lw = 0
                    for i in range(DCH):
                        lw = dma("sync", wvb[i][:, 0:512],
                                 wv[i * 128:(i + 1) * 128, hh * 512:(hh + 1) * 512])
                    wait("tensor", "sQ", lw)
                    if hh == 0:
                        wait("tensor", "sV", z_t)
                    vev = []
                    for tt in range(4):
                        bank = 4 + (tt % 2)
                        if len(vev) >= 2:
                            wait("tensor", "sV", vev[-2])
                        p_t = 0
                        for i in range(DCH):
                            p_t = pe(lambda t, i=i, tt=tt, bank=bank: t.matmul(
                                ps[bank][:, 0:512], xch[i][:, tt * 128:(tt + 1) * 128],
                                wvb[i][:, 0:512], start=(i == 0), stop=(i == DCH - 1)),
                                bump=(i == DCH - 1))
                        wait("vector", "sP", p_t)
                        vev.append(dve(lambda v, tt=tt, hh=hh, bank=bank: v.tensor_tensor(
                            bass.AP(vbl[tt], hh * 8 * 65, [[VW, 128], [65, 8], [1, 64]]),
                            ps[bank][:, 0:512],
                            cbvb[:, hh * 512:(hh + 1) * 512], ALU.add), bump=True))
                wait("gpsimd", "sV", cnt["sV"])
                for tt in range(4):
                    dma("gpsimd", vvs[m * 512 + tt * 128:m * 512 + (tt + 1) * 128, :], vbl[tt][:])
            barrier()
            if _TAPS:
                g_t = 0
                for i in range(DCH):
                    g_t = dma("gpsimd", taps["tap_q"][i * 128:(i + 1) * 128, :], bigA[i][:])
                g_t = dma("gpsimd", taps["tap_k"], kvs)
                g_t = dma("gpsimd", taps["tap_v"], vvs)
                wait("gpsimd", "sG", g_t)
        s1.close()

        s3 = ExitStack()
        ptl = [mk(s3, f"ptl{i}", [128, 512]) for i in range(16)]
        mbuf = mk(s3, "mbuf", [128, 1024], BF16)
        stage = mk(s3, "stage", [128, 2 * 512])
        if _STAGE >= 3:
            z_t = 0
            for i in range(DCH):
                z_t = dve(lambda v, i=i: v.memset(bigB[i][:], 0.0), bump=True)
            wait("tensor", "sV", z_t)
            stage_hist = []
            for g in range(NG):
                wait("sync", "sP", cnt["sP"])
                kb_t = 0
                for i in range(DCH):
                    kb_t = dma("sync", kbl[i][:],
                               kvs[i * 128:(i + 1) * 128, g * 512:(g + 1) * 512])
                for tt in range(4):
                    kb_t = dma("sync", vbl[tt][:],
                               vvs[g * 512 + tt * 128:g * 512 + (tt + 1) * 128, :])
                mk_t = dma("sync", mbuf[:], masks[g, :, :])
                wait("tensor", "sQ", kb_t)
                wait("vector", "sQ", mk_t)
                for (qs, qw_) in _suffix_chunks(g):
                    first_qc = (qs == 256 * g)
                    for hq in range(4):
                        exps = []
                        for hi in range(4):
                            h = 4 * hq + hi
                            r0 = (h % 2) * 64
                            tile = h // 2
                            for kt in range(4):
                                bank = (4 * hi + kt) % 4
                                if len(exps) >= 4:
                                    wait("tensor", "sA", exps[-4])
                                p_t = pe(lambda t, r0=r0, tile=tile, kt=kt, bank=bank, qs=qs, qw_=qw_:
                                         t.matmul(ps[bank][:, 0:qw_],
                                                  kbl[tile][r0:r0 + 64, kt * 128:(kt + 1) * 128],
                                                  bigA[tile][r0:r0 + 64, qs:qs + qw_],
                                                  start=True, stop=True), bump=True)
                                wait("scalar", "sP", p_t)
                                a_t = act(lambda s_, hi=hi, kt=kt, bank=bank, qw_=qw_:
                                          s_.activation(ptl[4 * hi + kt][:, 0:qw_],
                                                        ps[bank][:, 0:qw_],
                                                        AF.Exp, scale=float(1.0 / 32.0)),
                                          bump=True)
                                exps.append(a_t)
                        gate = ("sA", exps[-1])
                        if first_qc:
                            wait("vector", "sA", exps[-1])
                            m_t = 0
                            for hi in range(4):
                                for kt in range(4):
                                    for qt in range(2):
                                        m_t = dve(lambda v, hi=hi, kt=kt, qt=qt: v.tensor_mul(
                                            ptl[4 * hi + kt][:, qt * 128:(qt + 1) * 128],
                                            ptl[4 * hi + kt][:, qt * 128:(qt + 1) * 128],
                                            mbuf[:, (2 * kt + qt) * 128:(2 * kt + qt + 1) * 128]),
                                            bump=True)
                            gate = ("sV", m_t)
                        pe_drain()
                        for hi in range(4):
                            h = 4 * hq + hi
                            bank = 4 + hi
                            if first_qc:
                                wait("tensor", gate[0], gate[1])
                            else:
                                wait("tensor", "sA", exps[4 * hi + 3])
                            p_t = 0
                            for kt in range(4):
                                p_t = pe(lambda t, h=h, hi=hi, kt=kt, bank=bank, qw_=qw_:
                                         t.matmul(ps[bank][0:65, 0:qw_],
                                                  vbl[kt][:, h * 65:h * 65 + 65],
                                                  ptl[4 * hi + kt][:, 0:qw_],
                                                  start=(kt == 0), stop=(kt == 3)),
                                         bump=(kt == 3))
                            wait("vector", "sP", p_t)
                            dve(lambda v, h=h, bank=bank, qs=qs, qw_=qw_: v.tensor_tensor(
                                bigB[h // 2][(h % 2) * 64:(h % 2) * 64 + 64, qs:qs + qw_],
                                bigB[h // 2][(h % 2) * 64:(h % 2) * 64 + 64, qs:qs + qw_],
                                ps[bank][0:64, 0:qw_], ALU.add), bump=True)
                            slot = len(stage_hist) % 2
                            if len(stage_hist) >= 2:
                                wait("vector", "sG", stage_hist[-2])
                            v_t = dve(lambda v, bank=bank, slot=slot, qw_=qw_: v.tensor_copy(
                                stage[64:65, slot * 512:slot * 512 + qw_],
                                ps[bank][64:65, 0:qw_]), bump=True)
                            wait("gpsimd", "sV", v_t)
                            g_t = dma("gpsimd", dens[h, g, qs:qs + qw_],
                                      bass.AP(stage, 64 * 1024 + slot * 512, [[1024, 1], [1, qw_]]))
                            stage_hist.append(g_t)
                        wait("tensor", "sV", cnt["sV"])
                        pe_drain()
            barrier()
        s3.close()

        s4 = ExitStack()
        st["rows"] = mk(s4, "rows4", [1, 3 * 512])
        st["mub"] = mk(s4, "mub4", [128, 512])
        st["rb"] = mk(s4, "rb4", [128, 512])
        den = mk(s4, "den", [16, TOWN])
        bcst = mk(s4, "bcst", [128, TOWN])
        wob = mk(s4, "wob", [128, 2 * 1024])
        xrt = kbl
        sqt = [vbl[i % 4][:, (i // 4) * 512:(i // 4) * 512 + 512] for i in range(DCH)]
        dtmp = bcst
        if _STAGE >= 4:
            d_t = dma("sync", den[:, :], bass.AP(dens.tensor, 0, [[NG * TOWN, 16], [1, TOWN]]))
            wait("vector", "sQ", d_t)
            for g in range(1, NG):
                w_g = TOWN - 256 * g
                wait("sync", "sV", cnt["sV"])
                d_t = dma("sync", dtmp[0:16, 0:w_g],
                          bass.AP(dens.tensor, g * TOWN + 256 * g, [[NG * TOWN, 16], [1, w_g]]))
                wait("vector", "sQ", d_t)
                dve(lambda v, g=g, w_g=w_g: v.tensor_tensor(
                    den[0:16, 256 * g:TOWN], den[0:16, 256 * g:TOWN],
                    dtmp[0:16, 0:w_g], ALU.add), bump=True)
            v_t = dve(lambda v: v.reciprocal(den[:, :], den[:, :]), bump=True)
            if _TAPS:
                wait("gpsimd", "sV", v_t)
                g_t = 0
                for i in range(DCH):
                    g_t = dma("gpsimd", taps["tap_num"][i * 128:(i + 1) * 128, :], bigB[i][:])
                g_t = dma("gpsimd", taps["tap_den"], den[:, :])
                wait("gpsimd", "sG", g_t)
            wait("gpsimd", "sV", v_t)
            for j in range(DCH):
                wait("gpsimd", "sV", cnt["sV"])
                dma("gpsimd", bcst[0:64, :],
                    bass.AP(den, (2 * j) * TOWN, [[TOWN, 1], [0, 64], [1, TOWN]]))
                b_t = dma("gpsimd", bcst[64:128, :],
                          bass.AP(den, (2 * j + 1) * TOWN, [[TOWN, 1], [0, 64], [1, TOWN]]))
                wait("vector", "sG", b_t)
                dve(lambda v, j=j: v.tensor_mul(bigB[j][:], bigB[j][:], bcst[:]), bump=True)
            att_v = cnt["sV"]
            if _TAPS:
                wait("gpsimd", "sV", att_v)
                g_t = 0
                for i in range(DCH):
                    g_t = dma("gpsimd", taps["tap_att"][i * 128:(i + 1) * 128, :], bigB[i][:])
                wait("gpsimd", "sG", g_t)
            wait("tensor", "sV", att_v)
            for lc in range(NLCH):
                wait("sync", "sG", cnt["sG"])
                wait("sync", "sP", cnt["sP"])
                wait("sync", "sV", cnt["sV"])
                xw = 0
                for i in range(DCH):
                    xw = dma("sync", xrt[i][:],
                             bass.AP(xt.tensor, i * 128 * T + lc * 1024,
                                     [[T, 128], [512, 2], [1, 256]]))
                ph = []
                for o in range(DCH):
                    sl = (o % 2) * 1024
                    if len(ph) >= 2:
                        wait("sync", "sP", ph[-2])
                    lw = dma("sync", wob[:, sl:sl + 1024],
                             bass.AP(wo.tensor, o * 128, [[D, 128], [128 * D, 8], [1, 128]]))
                    wait("tensor", "sQ", lw)
                    if o >= 2:
                        wait("tensor", "sV", cnt["sV"])
                    p_t = 0
                    for i in range(DCH):
                        p_t = pe(lambda t, i=i, o=o, sl=sl, lc=lc: t.matmul(
                            ps[o % 2][:, 0:512], wob[:, sl + i * 128:sl + (i + 1) * 128],
                            bigB[i][:, lc * 512:(lc + 1) * 512],
                            start=(i == 0), stop=(i == DCH - 1)), bump=(i == DCH - 1))
                    ph.append(p_t)
                    if o == 0:
                        wait("vector", "sQ", xw)
                    wait("vector", "sP", p_t)
                    dve(lambda v, o=o: v.tensor_scalar(
                        sqt[o], ps[o % 2][:, 0:512],
                        cb[:, 3 * DCH + o:3 * DCH + o + 1], None, ALU.add))
                    v_t = dve(lambda v, o=o: v.tensor_tensor(
                        xrt[o][:], sqt[o], xrt[o][:], ALU.add), bump=True)
                    wait("gpsimd", "sV", v_t)
                    dma("gpsimd", xr2s[o * 128:(o + 1) * 128, lc * 512:(lc + 1) * 512], xrt[o][:])
                if not _NOLN4:
                    g_t = ln_stats(lambda i: xrt[i][:], lambda i: sqt[i], 512)
                    wait("vector", "sG", g_t)
                    for i in range(DCH):
                        dve(lambda v, i=i: v.tensor_sub(xrt[i][:], xrt[i][:], st["mub"][:, 0:512]))
                        dve(lambda v, i=i, lc=lc: v.tensor_mul(
                            bigA[i][:, lc * 512:(lc + 1) * 512], xrt[i][:], st["rb"][:, 0:512]),
                            bump=True)
            barrier()
            if _TAPS:
                g_t = 0
                for i in range(DCH):
                    g_t = dma("gpsimd", taps["tap_h0"][i * 128:(i + 1) * 128, :], bigA[i][:])
                g_t = dma("gpsimd", taps["tap_xr2"], xr2s)
                wait("gpsimd", "sG", g_t)
        s4.close()

        s5 = ExitStack()
        st["rows"] = mk(s5, "rows5", [1, 3 * 512])
        st["mub"] = mk(s5, "mub5", [128, 512])
        st["rb"] = mk(s5, "rb5", [128, 512])
        wff = [mk(s5, f"wff{i}", [128, 1024]) for i in range(DCH)]
        qwchS = [vbl[0][:, 0:1024].bitcast(I32), vbl[1][:, 0:1024].bitcast(I32)]
        scchS = [vbl[2][:, 0:1024], vbl[3][:, 0:1024]]
        y1 = kbl
        lasbT = mk(s5, "lasbT", [128, DCH * R])
        lbsbT = mk(s5, "lbsbT", [R, DCH * 128])
        lasb = [lasbT[:, i * R:(i + 1) * R] for i in range(DCH)]
        lbsb = [lbsbT[0:R, o * 128:(o + 1) * 128] for o in range(DCH)]
        t1sb = vbl[0][0:R, 0:512]

        def qlora(midx, in_tiles_fn, out_cb):
            wait("sync", "sP", cnt["sP"])
            wait("sync", "sV", cnt["sV"])
            deq_hist = []
            for o in range(DCH):
                sl = o % 2
                wait("sync", "sV", cnt["sV"])
                lw = dma("sync", qwchS[sl],
                         bass.AP(qw.tensor, midx * D * D + o * 128,
                                 [[D, 128], [128 * D, 8], [1, 128]]))
                lw = dma("sync", scchS[sl],
                         bass.AP(sce.tensor, midx * D * D + o * 128,
                                 [[D, 128], [128 * D, 8], [1, 128]]))
                wait("vector", "sQ", lw)
                if o == 0:
                    wait("vector", "sP", cnt["sP"])
                v_t = 0
                for i in range(DCH):
                    v_t = dve(lambda v, i=i, o=o, sl=sl: v.scalar_tensor_tensor(
                        wff[i][:, o * 128:(o + 1) * 128],
                        qwchS[sl][:, i * 128:(i + 1) * 128], 7.5,
                        scchS[sl][:, i * 128:(i + 1) * 128],
                        ALU.subtract, ALU.mult), bump=True)
                deq_hist.append(v_t)
            wait("sync", "sV", cnt["sV"])
            lw = 0
            for i in range(DCH):
                lw = dma("sync", lasb[i],
                         bass.AP(lat.tensor, midx * D * R + i * 128 * R, [[R, 128], [1, R]]))
            for o in range(DCH):
                lw = dma("sync", lbsb[o],
                         bass.AP(lbt.tensor, midx * R * D + o * 128, [[D, R], [1, 128]]))
            wait("tensor", "sQ", lw)
            wait("tensor", "sV", cnt["sV"])
            wait("tensor", "sA", cnt["sA"])
            if _TAPS and midx == int(_os.environ.get('TAPMIDX', '0')):
                wait("gpsimd", "sV", cnt["sV"])
                for i in range(DCH):
                    g_t = dma("gpsimd", taps["tap_wff"][i * 128:(i + 1) * 128, :], wff[i][:])
                wait("gpsimd", "sG", g_t)
            prev = {}
            for lc in range(NLCH):
                tiles = in_tiles_fn(lc)
                wait("tensor", "sV", cnt["sV"])
                if "mu" in prev:
                    wait("tensor", "sV", prev["mu"])
                p_t = 0
                for i in range(DCH):
                    p_t = pe(lambda t, i=i, ap_=tiles[i]: t.matmul(
                        ps[6][0:R, 0:512], lasb[i], ap_,
                        start=(i == 0), stop=(i == DCH - 1)), bump=(i == DCH - 1))
                wait("vector", "sP", p_t)
                v_t = dve(lambda v: v.tensor_scalar(t1sb, ps[6][0:R, 0:512], 7.5, None, ALU.mult),
                          bump=True)
                if _TAPS and midx == int(_os.environ.get('TAPMIDX', '0')) and lc == 0:
                    wait("gpsimd", "sV", v_t)
                    g_t2 = dma("gpsimd", taps["tap_t1"], t1sb)
                    wait("gpsimd", "sG", g_t2)
                for o in range(DCH):
                    if o in prev:
                        wait("tensor", "sV", prev[o])
                    if o == 6:
                        wait("tensor", "sV", v_t)
                    if o == 7 and "r1" in prev:
                        wait("tensor", "sV", prev["r1"])
                    p_t = 0
                    for i in range(DCH):
                        p_t = pe(lambda t, i=i, o=o, ap_=tiles[i]: t.matmul(
                            ps[o][:, 0:512], wff[i][:, o * 128:(o + 1) * 128], ap_,
                            start=(i == 0), stop=(i == DCH - 1)),
                            bump=(i == DCH - 1 and o == DCH - 1))
                pe_drain()
                wait("tensor", "sV", v_t)
                p_t = 0
                for o in range(DCH):
                    p_t = pe(lambda t, o=o: t.matmul(
                        ps[o][:, 0:512], lbsb[o], t1sb,
                        start=False, stop=True), bump=(o == DCH - 1))
                pe_drain()
                wait("vector", "sP", p_t)
                prev = out_cb(lc, p_t) or {}

        for blk in range(3):
            def in_h(lc):
                return [bigA[i][:, lc * 512:(lc + 1) * 512] for i in range(DCH)]

            def out1(lc, p_t, blk=blk):
                track = {}
                wait("vector", "sA", cnt["sA"])
                for o in range(DCH):
                    track[o] = dve(lambda v, o=o, blk=blk: v.tensor_scalar(
                        y1[o][:], ps[o][:, 0:512], float(2.0 / 15.0),
                        cfb[:, (2 * blk) * DCH + o:(2 * blk) * DCH + o + 1],
                        ALU.mult, ALU.add), bump=True)
                g_t = ln_stats(lambda i: y1[i][:],
                               lambda i, lc=lc: bigB[i][:, lc * 512:(lc + 1) * 512], 512)
                track["mu"] = stats_mu[0]
                track["r1"] = stats_r1[0]
                wait("vector", "sG", g_t)
                v_t = 0
                for i in range(DCH):
                    dve(lambda v, i=i: v.tensor_sub(y1[i][:], y1[i][:], st["mub"][:, 0:512]))
                    v_t = dve(lambda v, i=i: v.tensor_mul(y1[i][:], y1[i][:], st["rb"][:, 0:512]),
                              bump=(i == DCH - 1))
                wait("scalar", "sV", v_t)
                for i in range(DCH):
                    act(lambda s_, i=i, lc=lc, blk=blk: s_.activation(
                        bigB[i][:, lc * 512:(lc + 1) * 512], y1[i][:], AF.Gelu,
                        bias=cb3[:, blk * DCH + i:blk * DCH + i + 1],
                        scale=cg3[:, blk * DCH + i:blk * DCH + i + 1]), bump=(i == DCH - 1))
                return track

            def in_y2(lc):
                return [bigB[i][:, lc * 512:(lc + 1) * 512] for i in range(DCH)]

            def out2(lc, p_t, blk=blk):
                track = {}
                wait("vector", "sA", cnt["sA"])
                for o in range(DCH):
                    dve(lambda v, o=o, blk=blk: v.tensor_scalar(
                        y1[o][:], ps[o][:, 0:512], float(2.0 / 15.0),
                        cfb[:, (2 * blk + 1) * DCH + o:(2 * blk + 1) * DCH + o + 1],
                        ALU.mult, ALU.add))
                    track[o] = dve(lambda v, o=o, lc=lc: v.tensor_tensor(
                        bigA[o][:, lc * 512:(lc + 1) * 512],
                        bigA[o][:, lc * 512:(lc + 1) * 512], y1[o][:], ALU.add), bump=True)
                return track

            if _STAGE >= 5:
                qlora(2 * blk, in_h, out1)
                if _TAPS and blk == int(_os.environ.get('TAPBLK', '-1')):
                    wait("gpsimd", "sA", cnt["sA"])
                    wait("gpsimd", "sV", cnt["sV"])
                    for i in range(DCH):
                        g_t = dma("gpsimd", taps["tap_y2"][i * 128:(i + 1) * 128, :], bigB[i][:])
                    wait("gpsimd", "sG", g_t)
                qlora(2 * blk + 1, in_y2, out2)
                if _TAPS and blk == int(_os.environ.get('TAPBLK', '-1')):
                    wait("gpsimd", "sV", cnt["sV"])
                    for i in range(DCH):
                        g_t = dma("gpsimd", taps["tap_h0"][i * 128:(i + 1) * 128, :], bigA[i][:])
                    wait("gpsimd", "sG", g_t)
                barrier()
        if _TAPS and _STAGE >= 5:
            g_t = 0
            for i in range(DCH):
                g_t = dma("gpsimd", taps["tap_y2"][i * 128:(i + 1) * 128, :], bigA[i][:])
            wait("gpsimd", "sG", g_t)

        st_t = 0
        if _STAGE >= 5:
            for lc in range(NLCH):
                wait("sync", "sG", cnt["sG"])
                xw = 0
                for i in range(DCH):
                    xw = dma("sync", y1[i][:],
                             xr2s[i * 128:(i + 1) * 128, lc * 512:(lc + 1) * 512])
                wait("vector", "sQ", xw)
                v_t = 0
                for i in range(DCH):
                    v_t = dve(lambda v, i=i, lc=lc: v.tensor_tensor(
                        y1[i][:], y1[i][:], bigA[i][:, lc * 512:(lc + 1) * 512], ALU.add),
                        bump=True)
                wait("gpsimd", "sV", v_t)
                for i in range(DCH):
                    st_t = dma("gpsimd", out_t[i * 128:(i + 1) * 128, lc * 512:(lc + 1) * 512],
                               y1[i][:])
            wait("gpsimd", "sG", st_t)
        else:
            z_t = dve(lambda v: v.memset(bigB[0][:, 0:512], 0.0), bump=True)
            wait("gpsimd", "sV", z_t)
            for i in range(DCH):
                for lc in range(NLCH):
                    st_t = dma("gpsimd", out_t[i * 128:(i + 1) * 128, lc * 512:(lc + 1) * 512],
                               bigB[0][:, 0:512])
            wait("gpsimd", "sG", st_t)
        s5.close()
        s13.close()

        @block.sync
        def _(e):
            for f in plan["sync"]:
                f(e)

        @block.gpsimd
        def _(e):
            for f in plan["gpsimd"]:
                f(e)

        @block.tensor
        def _(e):
            for f in plan["tensor"]:
                f(e)

        @block.vector
        def _(e):
            for f in plan["vector"]:
                f(e)

        @block.scalar
        def _(e):
            for f in plan["scalar"]:
                f(e)

    build_nc._cnt = dict(cnt)
    return nc


def _prep_inputs(inputs):
    x = np.asarray(inputs["L0_attn_x"], dtype=np.float32)
    wqw = np.asarray(inputs["L0_attn_Wq_w"], dtype=np.float32)
    wkw = np.asarray(inputs["L0_attn_Wk_w"], dtype=np.float32)
    wvw = np.asarray(inputs["L0_attn_Wv_w"], dtype=np.float32)
    wow = np.asarray(inputs["L0_attn_Wo_w"], dtype=np.float32)
    bq = np.asarray(inputs["L0_attn_Wq_b"], dtype=np.float32)
    bk = np.asarray(inputs["L0_attn_Wk_b"], dtype=np.float32)
    bv = np.asarray(inputs["L0_attn_Wv_b"], dtype=np.float32)
    bo = np.asarray(inputs["L0_attn_Wo_b"], dtype=np.float32)
    qwq = np.asarray(inputs["L0_ffn_qweights"], dtype=np.int32)
    scl = np.asarray(inputs["L0_ffn_scales"], dtype=np.float32)
    fbi = np.asarray(inputs["L0_ffn_biases"], dtype=np.float32)
    la = np.asarray(inputs["L0_ffn_lora_a"], dtype=np.float32)
    lb = np.asarray(inputs["L0_ffn_lora_b"], dtype=np.float32)
    lg = np.asarray(inputs["L0_ffn_ln_gamma"], dtype=np.float32)
    lbe = np.asarray(inputs["L0_ffn_ln_beta"], dtype=np.float32)

    wq_t = np.ascontiguousarray(wqw.T)
    wk_t = np.ascontiguousarray(wkw.T)
    wv_t = np.ascontiguousarray(wvw.T)
    wo_t = np.ascontiguousarray(wow.T)
    bqkvo = np.ascontiguousarray(np.concatenate([bq, bk, bv, bo]).reshape(4 * DCH, 128).T)
    bvb = np.ascontiguousarray(np.broadcast_to(bv[None, :], (128, D)))
    qw_t = np.ascontiguousarray(qwq.transpose(0, 2, 1))
    sce = np.ascontiguousarray(np.repeat(scl, 16, axis=2).transpose(0, 2, 1)) if scl.shape[2] == 64 else None
    sce = np.ascontiguousarray(np.repeat(scl, 16, axis=2).transpose(0, 2, 1))
    fb_t = np.ascontiguousarray(fbi.reshape(6 * DCH, 128).T)
    lat = np.ascontiguousarray(la.transpose(0, 2, 1))
    lbt = np.ascontiguousarray(lb.transpose(0, 2, 1))
    g3 = np.ascontiguousarray(lg.reshape(3 * DCH, 128).T)
    be3 = np.ascontiguousarray(lbe.reshape(3 * DCH, 128).T)
    consts = np.zeros((128, 2), dtype=np.float32)
    consts[:, 0] = EPS
    consts[:, 1] = 1.0 / D

    import ml_dtypes
    in_maps = []
    for c in range(8):
        b, p = c // 2, c % 2
        perm = np.empty(T, dtype=np.int64)
        for m in range(NCH):
            perm[m * 512:m * 512 + 256] = np.arange((2 * m + p) * 256, (2 * m + p) * 256 + 256)
            perm[m * 512 + 256:(m + 1) * 512] = np.arange((2 * m + 1 - p) * 256,
                                                          (2 * m + 1 - p) * 256 + 256)
        xt_c = np.ascontiguousarray(x[b].T[:, perm])
        mk_ = np.zeros((NG, 128, 1024), dtype=np.float32)
        for g in range(NG):
            kcols = perm[g * 512:(g + 1) * 512]
            qpos = (2 * g + p) * 256 + np.arange(256)
            for kt in range(4):
                for qt in range(2):
                    kk = kcols[kt * 128:(kt + 1) * 128]
                    qq = qpos[qt * 128:(qt + 1) * 128]
                    mk_[g, :, (2 * kt + qt) * 128:(2 * kt + qt + 1) * 128] = (
                        qq[None, :] >= kk[:, None]).astype(np.float32)
        in_maps.append(dict(
            xt=xt_c, wq=wq_t, wk=wk_t, wv=wv_t, wo=wo_t, bqkvo=bqkvo, bvb=bvb,
            qw=qw_t, sce=sce, fb=fb_t, lat=lat, lbt=lbt, g3=g3, be3=be3,
            masks=mk_.astype(ml_dtypes.bfloat16), consts=consts))
    return in_maps


def _gather(results):
    out = np.empty((B, T, D), dtype=np.float32)
    for c in range(8):
        b, p = c // 2, c % 2
        ot = np.asarray(results[c]["out_t"])
        own = np.empty(TOWN, dtype=np.int64)
        for m in range(NCH):
            own[m * 256:(m + 1) * 256] = np.arange((2 * m + p) * 256, (2 * m + p) * 256 + 256)
        out[b, own, :] = ot.T
    return out


def kernel(**inputs):
    in_maps = _prep_inputs(inputs)
    nc = build_nc()
    res = run_bass_kernel_spmd(nc, in_maps, core_ids=list(range(8)))
    return _gather(res.results)
